# version 01
# baseline (speedup 1.0000x reference)
import numpy as np
import ml_dtypes
from contextlib import ExitStack
import concourse.bass as bass
import concourse.mybir as mybir
from concourse.bass_utils import run_bass_kernel_spmd

F32 = mybir.dt.float32
BF16 = mybir.dt.bfloat16
AF = mybir.ActivationFunctionType
ALU = mybir.AluOpType
AX = mybir.AxisListType

S = 2048
D = 1024
NCH = 8

ENGINES = ("tensor", "vector", "scalar", "gpsimd", "sync")
DMAQ = ("sync", "scalar", "gpsimd")
NDSEM = 8


class _Op:
    __slots__ = ("eng", "fn", "deps", "idx", "is_dma", "dsem", "dval", "inc",
                 "waits", "throttle", "tag", "wd")

    def __init__(self, eng, fn, idx, tag=None):
        self.eng = eng
        self.fn = fn
        self.idx = idx
        self.deps = set()
        self.is_dma = False
        self.dsem = None
        self.dval = 0
        self.inc = False
        self.waits = []
        self.throttle = None
        self.tag = tag
        self.wd = 0


class PB:
    def __init__(self, nc, st):
        self.nc = nc
        self.ops = []
        self.esem = {e: st.enter_context(nc.semaphore(f"es_{e}")) for e in ENGINES}
        self.dsem = {q: [st.enter_context(nc.semaphore(f"ds_{q}{i}")) for i in range(NDSEM)]
                     for q in DMAQ}
        self.dcnt = {q: 0 for q in DMAQ}
        self.dslot = {q: [0] * NDSEM for q in DMAQ}
        self.state = {}
        self.nbarrier = 0

    def _track(self, op, reads, writes):
        wdeps = set()
        rdeps = set()
        for k in reads:
            s = self.state.get(k)
            if s:
                wdeps |= s["w"]
        for k in writes:
            s = self.state.get(k)
            if s:
                wdeps |= s["w"]
                rdeps.update(s["r"])
        for k in writes:
            self.state[k] = {"w": {op}, "r": []}
        for k in reads:
            s = self.state.setdefault(k, {"w": set(), "r": []})
            s["r"].append(op)
        out = set()
        for d in wdeps:
            if d is op:
                continue
            if d.eng == op.eng and (d.is_dma == op.is_dma):
                if op.eng == "tensor" or d.is_dma or d.wd >= 256:
                    continue
            out.add(d)
        for d in rdeps:
            if d is op:
                continue
            if d.eng == op.eng and (d.is_dma == op.is_dma):
                continue
            out.add(d)
        op.deps = out

    def op(self, eng, fn, r=(), w=(), tag=None, wd=0):
        o = _Op(eng, fn, len(self.ops), tag)
        o.wd = wd
        self.ops.append(o)
        self._track(o, r, w)
        return o

    def dma(self, q, out_ap, in_ap, r=(), w=(), tag=None, **kw):
        slot = self.dcnt[q] % NDSEM
        sem = self.dsem[q][slot]
        prev = self.dslot[q][slot]
        o = _Op(q, None, len(self.ops), tag)
        o.is_dma = True
        o.dsem = sem
        o.dval = 16 * (prev + 1)
        if prev:
            o.throttle = (sem, 16 * prev)
        o.fn = (lambda e, out_ap=out_ap, in_ap=in_ap, kw=kw:
                e.dma_start(out=out_ap, in_=in_ap, **kw))
        self.ops.append(o)
        self.dcnt[q] += 1
        self.dslot[q][slot] = prev + 1
        self._track(o, r, w)
        return o

    def barrier(self, keep_prefix="W:"):
        self.nbarrier += 1
        last = {}
        for o in self.ops:
            if not o.is_dma:
                last[o.eng] = o
        for e in ENGINES:
            o = _Op(e, (lambda eng: eng.nop()), len(self.ops), f"bar{self.nbarrier}")
            self.ops.append(o)
            o.deps = set(d for en, d in last.items() if en != e)
            for q in ("sync", "scalar"):
                for i in range(NDSEM):
                    n = self.dslot[q][i]
                    if n:
                        o.waits.append((self.dsem[q][i], 16 * n))
        ns = {}
        for k, s in self.state.items():
            if k.startswith(keep_prefix):
                ns[k] = s
        self.state = ns

    def finalize(self, block):
        for o in self.ops:
            for d in o.deps:
                if not d.is_dma:
                    d.inc = True
        cnt = {e: 0 for e in ENGINES}
        upto = {}
        for o in self.ops:
            if not o.is_dma:
                if o.inc:
                    cnt[o.eng] += 1
                upto[o.idx] = cnt[o.eng]
        hw = {e: {} for e in ENGINES}
        for o in self.ops:
            need = {}
            for d in o.deps:
                if d.is_dma:
                    sem, val = d.dsem, d.dval
                else:
                    sem, val = self.esem[d.eng], upto[d.idx]
                if need.get(id(sem), (None, 0))[1] < val:
                    need[id(sem)] = (sem, val)
            if o.throttle:
                sem, val = o.throttle
                if need.get(id(sem), (None, 0))[1] < val:
                    need[id(sem)] = (sem, val)
            h = hw[o.eng]
            for _, (sem, val) in need.items():
                if h.get(id(sem), 0) < val:
                    o.waits.append((sem, val))
                    h[id(sem)] = val
        streams = {e: [] for e in ENGINES}
        for o in self.ops:
            streams[o.eng].append(o)

        def mk(ops):
            def run(eng):
                for o in ops:
                    for sem, val in o.waits:
                        eng.wait_ge(sem, val)
                    inst = o.fn(eng)
                    if o.is_dma:
                        inst.then_inc(o.dsem, 16)
                    elif o.inc:
                        inst.then_inc(self.esem[o.eng], 1)
            return run

        block.tensor(mk(streams["tensor"]))
        block.vector(mk(streams["vector"]))
        block.scalar(mk(streams["scalar"]))
        block.gpsimd(mk(streams["gpsimd"]))
        block.sync(mk(streams["sync"]))


class Ctx:
    def __init__(self, nc, st, pb):
        self.nc = nc
        self.st = st
        self.pb = pb
        self.dram = {}
        self.sb = {}
        self.dbg_names = []

    def dbg(self, name, ap, rkeys):
        if not self.cfg.get("debug"):
            return
        t = self.nc.dram_tensor("dbg_" + name, list(ap.shape), F32, kind="ExternalOutput")
        self.dbg_names.append("dbg_" + name)
        self.pb.dma("gpsimd", t[tuple(slice(0, s) for s in ap.shape)], ap, r=tuple(rkeys),
                    w=("dbg_" + name,), tag="dbg_" + name)

    def sbuf(self, name, p, f, dt, stack=None):
        t = (stack or self.st).enter_context(self.nc.sbuf_tensor(name, [p, f], dt,
                                                                 allow_name_mangling=True))
        self.sb[name] = t
        return t

    def psum(self, name, p, f, dt, stack=None):
        t = (stack or self.st).enter_context(self.nc.psum_tensor(name, [p, f], dt,
                                                                 allow_name_mangling=True))
        self.sb[name] = t
        return t


def vop(cx, fn, r=(), w=(), tag=None, wd=0):
    return cx.pb.op("vector", fn, r, w, tag, wd=wd)


def sop(cx, fn, r=(), w=(), tag=None, wd=0):
    return cx.pb.op("scalar", fn, r, w, tag, wd=wd)


def top(cx, fn, r=(), w=(), tag=None, wd=0):
    return cx.pb.op("tensor", fn, r, w, tag, wd=wd)


def build_program(cfg):
    nc = bass.Bass("TRN2", target_bir_lowering=False)
    st = ExitStack()
    pb = PB(nc, st)
    cx = Ctx(nc, st, pb)
    cx.cfg = cfg

    for name, (shape, dt) in cfg["inputs_spec"].items():
        dtt = {"float32": F32, "bfloat16": BF16}[dt]
        cx.dram[name] = nc.dram_tensor(name, list(shape), dtt, kind="ExternalInput")
    cx.dram["out"] = nc.dram_tensor("out", [D, S], F32, kind="ExternalOutput")
    cx.dram["scr0"] = nc.dram_tensor("scr0", [S, 1536], BF16, kind="Internal")
    cx.dram["scrk"] = nc.dram_tensor("scrk", [1, S], F32, kind="Internal")
    cx.dram["scrRp"] = nc.dram_tensor("scrRp", [4160, 25], F32, kind="Internal")
    cx.dram["scrV"] = nc.dram_tensor("scrV", [16640, 128], BF16, kind="Internal")
    cx.dram["scrC"] = nc.dram_tensor("scrC", [9, S], F32, kind="Internal")

    for c in range(NCH):
        cx.sbuf(f"xT{c}", 128, S, F32)
    for c in range(NCH):
        cx.sbuf(f"xTb{c}", 128, S, BF16)
    cx.sbuf("ones_f", 128, 128, F32)
    cx.sbuf("ones_bh", 1, 512, BF16)
    cx.sbuf("ident", 128, 128, BF16)
    cx.sbuf("identf", 128, 128, F32)
    cx.sbuf("eps_row", 1, 8, F32)

    block = st.enter_context(nc.Block())

    pb.op("vector", lambda e: e.memset(cx.sb["ones_f"][:, :], 1.0), w=("ones_f",))
    pb.op("vector", lambda e: e.memset(cx.sb["ones_bh"][:, :], 1.0), w=("ones_bh",))
    pb.op("vector", lambda e: e.memset(cx.sb["eps_row"][:, :], 1e-5), w=("eps_row",))
    pb.dma("gpsimd", cx.sb["ident"][:, :], cx.dram["c_ident"][:, :], w=("ident",))
    pb.dma("sync", cx.sb["identf"][:, :], cx.dram["c_ident"][:, :], w=("identf",))
    for c in range(NCH):
        pb.dma("sync", cx.sb[f"xT{c}"][:, :], cx.dram["xt_in"][128 * c:128 * (c + 1), :],
               w=(f"xT{c}",), tag=f"ldx{c}")

    for lname in cfg["layers"]:
        LAYERS[lname](cx)
        pb.barrier()

    for c in range(NCH):
        pb.dma("sync", cx.dram["out"][128 * c:128 * (c + 1), :], cx.sb[f"xT{c}"][:, :],
               r=(f"xT{c}",), w=("out",), tag=f"stx{c}")

    pb.finalize(block)
    st.close()
    return nc


def x_to_bf(cx):
    for c in range(NCH):
        sop(cx, lambda e, c=c: e.copy(cx.sb[f"xTb{c}"][:, :], cx.sb[f"xT{c}"][:, :]),
            r=(f"xT{c}",), w=(f"xTb{c}",), tag=f"xbf{c}")



def emit_postln(cx, rn, c0, W, pmean, psq_, prep_m, prep_s, tag="", gb=None, ro=0, dst=None):
    ones_col = cx.sb["ones_f"][:, 0:1]
    ones_row = cx.sb["ones_f"][0:1, :]
    (pm, pmk), (pq, pqk), (prm, prmk), (prs, prsk) = pmean, psq_, prep_m, prep_s
    sq = cx.sb[f"{rn}_tmp"]
    mrow = cx.sb[f"{rn}_rows"]
    tmp = cx.sb[f"{rn}_tmp"]
    for c in range(NCH):
        top(cx, lambda e, c=c: e.matmul(pm[0:1, 0:W], ones_col, cx.sb[f"{rn}{c}"][:, ro:ro + W],
                                        start=(c == 0), stop=(c == NCH - 1)),
            r=(f"{rn}{c}", "ones_f"), w=(pmk,), tag=f"lnm{tag}{c}")
    for c in range(NCH):
        sop(cx, lambda e, c=c: e.activation(sq[:, 0:W], cx.sb[f"{rn}{c}"][:, ro:ro + W], AF.Square),
            r=(f"{rn}{c}",), w=(f"{rn}_tmp",), tag=f"lnsq{tag}{c}")
        top(cx, lambda e, c=c: e.matmul(pq[0:1, 0:W], ones_col, sq[:, 0:W],
                                        start=(c == 0), stop=(c == NCH - 1)),
            r=(f"{rn}_tmp", "ones_f"), w=(pqk,), tag=f"lnqm{tag}{c}")
    vop(cx, lambda e: e.tensor_scalar(mrow[0:1, 0:W], pm[0:1, 0:W], -1.0 / D, None, ALU.mult),
        r=(pmk,), w=(f"{rn}_rows",), tag=f"lnmean{tag}")
    vop(cx, lambda e: e.tensor_tensor(mrow[0:1, W:2 * W], mrow[0:1, 0:W], mrow[0:1, 0:W], ALU.mult),
        r=(f"{rn}_rows",), w=(f"{rn}_rows",), tag=f"lnm2{tag}")
    vop(cx, lambda e: e.scalar_tensor_tensor(mrow[0:1, W:2 * W], pq[0:1, 0:W], 1.0 / D,
                                             mrow[0:1, W:2 * W], ALU.mult, ALU.subtract),
        r=(pqk, f"{rn}_rows"), w=(f"{rn}_rows",), tag=f"lnvar{tag}")
    sop(cx, lambda e: e.activation(mrow[0:1, 2 * W:3 * W], mrow[0:1, W:2 * W], AF.Sqrt,
                                   bias=cx.sb["eps_row"][0:1, 0:1]),
        r=(f"{rn}_rows", "eps_row"), w=(f"{rn}_rows2",), tag=f"lnsqrt{tag}")
    vop(cx, lambda e: e.reciprocal(mrow[0:1, 2 * W:3 * W], mrow[0:1, 2 * W:3 * W]),
        r=(f"{rn}_rows2",), w=(f"{rn}_rows2",), tag=f"lnrec{tag}")
    top(cx, lambda e: e.matmul(prm[:, 0:W], ones_row, mrow[0:1, 0:W], start=True, stop=True),
        r=(f"{rn}_rows", "ones_f"), w=(prmk,), tag=f"lnrepm{tag}")
    top(cx, lambda e: e.matmul(prs[:, 0:W], ones_row, mrow[0:1, 2 * W:3 * W], start=True, stop=True),
        r=(f"{rn}_rows2", "ones_f"), w=(prsk,), tag=f"lnreps{tag}")
    for c in range(NCH):
        vop(cx, lambda e, c=c: e.tensor_add(tmp[:, 0:W], cx.sb[f"{rn}{c}"][:, ro:ro + W], prm[:, 0:W]),
            r=(f"{rn}{c}", prmk), w=(f"{rn}_tmp",), tag=f"lnsub{tag}{c}")
        if dst is None:
            dget = lambda c2: (cx.sb[f"xT{c2}"][:, c0:c0 + W], f"xT{c2}")
        else:
            dget = dst
        if gb is None:
            vop(cx, lambda e, c=c: e.tensor_mul(dget(c)[0], tmp[:, 0:W], prs[:, 0:W]),
                r=(f"{rn}_tmp", prsk), w=(dget(c)[1],), wd=W, tag=f"lnout{tag}{c}")
        else:
            vop(cx, lambda e, c=c: e.tensor_mul(tmp[:, 0:W], tmp[:, 0:W], prs[:, 0:W]),
                r=(f"{rn}_tmp", prsk), w=(f"{rn}_tmp",), wd=W, tag=f"lnmul{tag}{c}")
            gk_, bk_ = gb[0], gb[1]
            vop(cx, lambda e, c=c, gk_=gk_, bk_=bk_:
                e.tensor_scalar(dget(c)[0], tmp[:, 0:W], gk_(c), bk_(c), ALU.mult, ALU.add),
                r=(f"{rn}_tmp", gb[2]), w=(dget(c)[1],), wd=W, tag=f"lnout{tag}{c}")


def alloc_ln_bufs(cx, sc, rn, W, rW=None):
    rW = rW or W
    cx.sbuf(f"{rn}_rows", 1, 3 * W, F32, stack=sc)
    cx.sbuf(f"{rn}_tmp", 128, W, F32, stack=sc)
    for c in range(NCH):
        cx.sbuf(f"{rn}{c}", 128, rW, F32, stack=sc)



def emit_l0_attn(cx):
    pb, nc = cx.pb, cx.nc
    H, HD, P = 16, 64, 256
    HALF = S // 2

    with ExitStack() as sc:
        wh = cx.sbuf("wh", 128, 2 * 1536, BF16, stack=sc)
        wpb = cx.sbuf("wpb", 128, 2 * 1024, BF16, stack=sc)
        posT = cx.sbuf("posT", 64, P, BF16, stack=sc)
        posL = cx.sbuf("posL", 64, 1, BF16, stack=sc)
        posR = cx.sbuf("posR", 64, 1, BF16, stack=sc)
        pb.dma("gpsimd", posT[:, :], cx.dram["l0_posT"][:, :], w=("posT",))
        pb.dma("gpsimd", posL[:, :], cx.dram["l0_posL"][:, :], w=("posL",))
        pb.dma("gpsimd", posR[:, :], cx.dram["l0_posR"][:, :], w=("posR",))

        x_to_bf(cx)

        k65 = cx.sbuf("k65", 65, S, BF16, stack=sc)
        qn = cx.sbuf("qn", 65, HALF, BF16, stack=sc)
        ql = cx.sbuf("ql", 65, HALF, BF16, stack=sc)
        qr = cx.sbuf("qr", 65, HALF, BF16, stack=sc)
        vh = cx.sbuf("vh", 128, 16 * 65, BF16, stack=sc)
        rp = cx.sbuf("rp", 128, 8 * P, F32, stack=sc)
        repr_ = cx.sbuf("repr", 128, 8 * 128, BF16, stack=sc)
        repl_ = cx.sbuf("repl", 128, 8 * 128, BF16, stack=sc)
        diag = cx.sbuf("diag", 128, 8 * 128, BF16, stack=sc)
        rolt = cx.sbuf("rolt", 128, 3 * 8 * 128, BF16, stack=sc)
        pt = cx.sbuf("pt", 128, 2 * 512, BF16, stack=sc)
        onesb = cx.sbuf("onesb", 128, 128, BF16, stack=sc)
        yden = cx.sbuf("yden", 1, 512, F32, stack=sc)
        ytmp = cx.sbuf("ytmp", 64, 512, F32, stack=sc)
        for c in range(NCH):
            cx.sbuf(f"yT{c}", 128, HALF, BF16, stack=sc)
        alloc_ln_bufs(cx, sc, "r0", 512)
        pb.op("vector", lambda e: e.memset(onesb[:, :], 1.0), w=("onesb",))
        pb.op("vector", lambda e: e.memset(k65[64:65, :], 1.0), w=("k65",))
        pb.op("vector", lambda e: e.memset(qn[64:65, :], 0.0), w=("qn",))
        pb.op("vector", lambda e: e.memset(vh[:, :], 1.0), w=("vh",))

        psA0 = cx.psum("psA0", 128, 1024, F32, stack=sc)
        psA1 = cx.psum("psA1", 128, 512, F32, stack=sc)
        psB = cx.psum("psB", 128, 1024, F32, stack=sc)
        psC = cx.psum("psC", 128, 512, F32, stack=sc)
        psD = cx.psum("psD", 128, 512, F32, stack=sc)
        psE = cx.psum("psE", 128, 1024, BF16, stack=sc)
        psCD = [(psC, "psC"), (psD, "psD")]

        scr = cx.dram["scr0"]

        for half in range(2):
            hb = half * HALF
            for h in range(16):
                hc, hp = h // 2, (h % 2) * 64
                ho = 1536 * (h % 2)
                pb.dma("gpsimd",
                       bass.AP(wh, ho, [[2 * 1536, 128], [192, 8], [1, 192]]),
                       bass.AP(cx.dram["l0_wqkv"], h * 1024 * 192,
                               [[192, 128], [128 * 192, 8], [1, 192]]),
                       w=(f"W:wh{h % 2}",), tag=f"ldwh{h}")
                for tb in range(4):
                    ps, pk = psCD[tb % 2]
                    for c in range(NCH):
                        top(cx, lambda e, c=c, tb=tb, ps=ps, ho=ho:
                            e.matmul(ps[0:64, 0:512],
                                     wh[:, ho + 192 * c + 64:ho + 192 * c + 128],
                                     cx.sb[f"xTb{c}"][:, 512 * tb:512 * tb + 512],
                                     start=(c == 0), stop=(c == NCH - 1)),
                            r=(f"W:wh{h % 2}", f"xTb{c}"), w=(pk,), tag=f"kp{h}_{tb}")
                    vop(cx, lambda e, tb=tb, ps=ps:
                        e.tensor_copy(k65[0:64, 512 * tb:512 * tb + 512], ps[0:64, 0:512]),
                        r=(pk,), w=("k65",), tag=f"ke{h}_{tb}")
                for tc in range(16):
                    ps, pk = psCD[tc % 2]
                    for c in range(NCH):
                        top(cx, lambda e, c=c, tc=tc, ps=ps, ho=ho:
                            e.matmul(ps[:, 0:64],
                                     cx.sb[f"xTb{c}"][:, 128 * tc:128 * tc + 128],
                                     wh[:, ho + 192 * c + 128:ho + 192 * c + 192],
                                     start=(c == 0), stop=(c == NCH - 1)),
                            r=(f"W:wh{h % 2}", f"xTb{c}"), w=(pk,), tag=f"vp{h}_{tc}")
                    vop(cx, lambda e, tc=tc, ps=ps:
                        e.tensor_copy(vh[:, 65 * tc:65 * tc + 64], ps[:, 0:64]),
                        r=(pk,), w=("vh",), tag=f"ve{h}_{tc}")
                for qb in range(2):
                    ps, pk = psCD[qb % 2]
                    for c in range(NCH):
                        top(cx, lambda e, c=c, qb=qb, ps=ps, ho=ho, hb=hb:
                            e.matmul(ps[0:64, 0:512],
                                     wh[:, ho + 192 * c:ho + 192 * c + 64],
                                     cx.sb[f"xTb{c}"][:, hb + 512 * qb:hb + 512 * qb + 512],
                                     start=(c == 0), stop=(c == NCH - 1)),
                            r=(f"W:wh{h % 2}", f"xTb{c}"), w=(pk,), tag=f"qp{h}_{qb}")
                    for dst, key in ((qn, "qn"), (ql, "ql"), (qr, "qr")):
                        vop(cx, lambda e, qb=qb, ps=ps, dst=dst:
                            e.tensor_copy(dst[0:64, 512 * qb:512 * qb + 512], ps[0:64, 0:512]),
                            r=(pk,), w=(key,), tag=f"qe{h}_{qb}")
                if half == 0 and h == 0:
                    cx.dbg("k65", k65[:, :], ("k65",))
                    cx.dbg("vh", vh[:, :], ("vh",))
                for qb in range(2):
                    for (pw, dst, key, tg) in ((posL, ql, "ql", "L"), (posR, qr, "qr", "R")):
                        ps, pk = psCD[0 if tg == "L" else 1]
                        top(cx, lambda e, qb=qb, ps=ps, pw=pw:
                            e.matmul(ps[0:1, 0:512], pw[:, 0:1],
                                     qn[0:64, 512 * qb:512 * qb + 512], start=True, stop=True),
                            r=("qn", "posL", "posR"), w=(pk,), tag=f"fr{tg}{h}_{qb}")
                        sop(cx, lambda e, qb=qb, ps=ps, dst=dst:
                            e.mul(dst[64:65, 512 * qb:512 * qb + 512], ps[0:1, 0:512], 2.0),
                            r=(pk,), w=(key,), tag=f"fre{tg}{h}_{qb}")
                if half == 0 and h == 0:
                    cx.dbg("qn", qn[:, :], ("qn",))
                    cx.dbg("ql", ql[:, :], ("ql",))
                    cx.dbg("qr", qr[:, :], ("qr",))
                for ml in range(8):
                    ps, pk = psCD[ml % 2]
                    top(cx, lambda e, ml=ml, ps=ps:
                        e.matmul(ps[:, 0:P], qn[0:64, 128 * ml:128 * ml + 128], posT[:, :],
                                 start=True, stop=True),
                        r=("qn", "posT"), w=(pk,), tag=f"rp{h}_{ml}")
                    vop(cx, lambda e, ml=ml, ps=ps:
                        e.tensor_copy(rp[:, P * ml:P * ml + P], ps[:, 0:P]),
                        r=(pk,), w=("rp",), tag=f"rpe{h}_{ml}")
                    vop(cx, lambda e, ml=ml:
                        e.tensor_scalar(repr_[:, 128 * ml:128 * ml + 128], onesb[:, :],
                                        rp[:, P * ml + 127:P * ml + 128], None, ALU.mult),
                        r=("rp", "onesb"), w=("repr",), tag=f"rr{h}_{ml}")
                    vop(cx, lambda e, ml=ml:
                        e.tensor_scalar(repl_[:, 128 * ml:128 * ml + 128], onesb[:, :],
                                        rp[:, P * ml + 128:P * ml + 129], None, ALU.mult),
                        r=("rp", "onesb"), w=("repl",), tag=f"rl{h}_{ml}")
                if half == 0 and h == 0:
                    cx.dbg("rp", rp[:, :], ("rp",))
                    cx.dbg("repr", repr_[:, :], ("repr",))
                def wdma(dst_off, src_t, src_off, width, src_row, keys, w, tg):
                    for rep in range(2):
                        pb.dma("gpsimd",
                               bass.AP(scr, dst_off + 256 * rep,
                                       [[1536, 128], [128 * 1536, 8], [1, width]]),
                               bass.AP(src_t, src_off,
                                       [[8 * src_row, 128], [src_row, 8], [1, width]]),
                               r=keys, w=(w,), tag=f"{tg}_{rep}")
                wdma(hb * 1536, rp, 0, 256, P, ("rp",), "scrb", f"sb{h}")
                wdma(hb * 1536 + 512, rp, 0, 128, P, ("rp",), "scrR", f"sRa{h}")
                wdma(hb * 1536 + 512 + 128, repr_, 0, 128, 128, ("repr",), "scrR", f"sRb{h}")
                wdma(hb * 1536 + 1024, repl_, 0, 128, 128, ("repl",), "scrL", f"sLa{h}")
                wdma(hb * 1536 + 1024 + 128, rp, 128, 128, P, ("rp",), "scrL", f"sLb{h}")
                for vi, (voff, key) in enumerate(((0, "scrb"), (512, "scrR"), (1024, "scrL"))):
                    for par in range(2):
                        extra = 0 if vi == 0 else (128 if par == 0 else -128)
                        off = (hb + 128 * par) * 1536 + voff + extra + 256
                        pb.dma("scalar",
                               bass.AP(diag, 128 * par, [[8 * 128, 128], [2 * 128, 4], [1, 128]]),
                               bass.AP(scr, off, [[1536 - 1, 128], [2 * 128 * 1536, 4], [1, 128]]),
                               r=(key,), w=("diag",), tag=f"dg{h}_{vi}_{par}")
                    for ml in range(8):
                        top(cx, lambda e, ml=ml:
                            e.transpose(psE[:, 0:128], diag[:, 128 * ml:128 * ml + 128],
                                        cx.sb["ident"][:, :]),
                            r=("diag", "ident"), w=("psE",), tag=f"tp{h}_{vi}_{ml}")
                        vop(cx, lambda e, ml=ml, vi=vi:
                            e.tensor_copy(rolt[:, 1024 * vi + 128 * ml:1024 * vi + 128 * ml + 128],
                                          psE[:, 0:128]),
                            r=("psE",), w=("rolt",), tag=f"te{h}_{vi}_{ml}")
                if half == 0 and h == 0:
                    cx.dbg("rolt", rolt[:, :], ("rolt",))
                    cx.dbg("diagL", diag[:, :], ("diag",))
                for blk in range(2):
                    sb0 = 512 * blk
                    mg0 = half * 8 + blk * 4
                    for n in range(16):
                        pa, pak = (psA0, "psA0") if n % 2 == 0 else (psA1, "psA1")
                        near = [j for j in range(4) if abs((mg0 + j) - n) <= 1]
                        rfar = [j for j in range(4) if (mg0 + j) < n - 1]
                        lfar = [j for j in range(4) if (mg0 + j) > n + 1]
                        segs = []
                        if rfar:
                            segs.append((rfar[0], len(rfar), qr))
                        if near:
                            segs.append((near[0], len(near), qn))
                        if lfar:
                            segs.append((lfar[0], len(lfar), ql))
                        nseg = len(segs)
                        for si, (j0, ln, qv) in enumerate(segs):
                            top(cx, lambda e, n=n, pa=pa, j0=j0, ln=ln, qv=qv, sb0=sb0:
                                e.matmul(pa[:, 128 * j0:128 * (j0 + ln)],
                                         k65[:, 128 * n:128 * n + 128],
                                         qv[:, sb0 + 128 * j0:sb0 + 128 * (j0 + ln)],
                                         start=True, stop=True),
                                r=("k65", "ql", "qn", "qr"), w=(pak,), tag=f"sc{h}_{blk}_{n}_{si}")
                        for j in near:
                            mg = mg0 + j
                            ml = mg - half * 8
                            vi = 0 if mg == n else (1 if mg == n - 1 else 2)
                            vop(cx, lambda e, j=j, ml=ml, vi=vi, pa=pa:
                                e.scalar_tensor_tensor(
                                    pa[:, 128 * j:128 * j + 128],
                                    rolt[:, 1024 * vi + 128 * ml:1024 * vi + 128 * ml + 128],
                                    2.0, pa[:, 128 * j:128 * j + 128], ALU.mult, ALU.add),
                                r=("rolt", pak), w=(pak,), tag=f"ra{h}_{blk}_{n}_{j}")
                        sop(cx, lambda e, n=n, pa=pa:
                            e.activation(pt[:, 512 * (n % 2):512 * (n % 2) + 512],
                                         pa[:, 0:512], AF.Exp, scale=0.125),
                            r=(pak,), w=(f"pt{n % 2}",), tag=f"ex{h}_{blk}_{n}")
                        if half == 0 and h == 0 and blk == 0 and n in (2, 3, 9):
                            cx.dbg(f"pt{n}", pt[:, 512 * (n % 2):512 * (n % 2) + 512],
                                   (f"pt{n % 2}",))
                        top(cx, lambda e, n=n:
                            e.matmul(psB[0:65, 0:512], vh[:, 65 * n:65 * n + 65],
                                     pt[:, 512 * (n % 2):512 * (n % 2) + 512],
                                     start=(n == 0), stop=(n == 15)),
                            r=(f"pt{n % 2}", "vh"), w=("psB",), tag=f"pv{h}_{blk}_{n}")
                    vop(cx, lambda e: e.reciprocal(yden[0:1, :], psB[64:65, 0:512]),
                        r=("psB",), w=("yden",), tag=f"rc{h}_{blk}")
                    top(cx, lambda e: e.matmul(psC[0:64, 0:512], cx.sb["ones_f"][0:1, 0:64],
                                               yden[0:1, :], start=True, stop=True),
                        r=("yden", "ones_f"), w=("psC",), tag=f"rd{h}_{blk}")
                    vop(cx, lambda e: e.tensor_copy(ytmp[:, :], psB[0:64, 0:512]),
                        r=("psB",), w=("ytmp",), tag=f"yc{h}_{blk}")
                    vop(cx, lambda e, hc=hc, hp=hp, sb0=sb0:
                        e.tensor_mul(cx.sb[f"yT{hc}"][hp:hp + 64, sb0:sb0 + 512],
                                     ytmp[:, :], psC[0:64, 0:512]),
                        r=("ytmp", "psC"), w=(f"yT{hc}",), tag=f"yn{h}_{blk}")
                    if half == 0 and h == 0 and blk == 0:
                        cx.dbg("ytmp", ytmp[:, :], ("ytmp",))
                        cx.dbg("yden", yden[0:1, :], ("yden",))

            for blk in range(2):
                sb0 = 512 * blk
                for wave in range(2):
                    pslist = [(psA0[:, 0:512], "psA0"), (psA0[:, 512:1024], "psA0"),
                              (psB[:, 0:512], "psB"), (psB[:, 512:1024], "psB")]
                    for k in range(NCH):
                        wslot = k % 2
                        pb.dma("gpsimd", wpb[:, 1024 * wslot:1024 * wslot + 1024],
                               cx.dram["l0_wp_t"][128 * k:128 * (k + 1), :],
                               w=(f"W:wpb{wslot}",), tag=f"ldwp{half}_{blk}_{wave}_{k}")
                        for ci in range(4):
                            c = 4 * wave + ci
                            pps, ppk = pslist[ci]
                            top(cx, lambda e, c=c, k=k, sb0=sb0, pps=pps, wslot=wslot:
                                e.matmul(pps, wpb[:, 1024 * wslot + 128 * c:1024 * wslot + 128 * c + 128],
                                         cx.sb[f"yT{k}"][:, sb0:sb0 + 512],
                                         start=(k == 0), stop=(k == NCH - 1)),
                                r=(f"W:wpb{wslot}", f"yT{k}"), w=(ppk,), tag=f"fp{half}_{blk}_{c}")
                    if half == 0 and blk == 0 and wave == 0:
                        vop(cx, lambda e: e.tensor_copy(cx.sb["r0_tmp"][:, :], pslist[0][0]),
                            r=(pslist[0][1],), w=("r0_tmp",), tag="dbgpp")
                        cx.dbg("proj0", cx.sb["r0_tmp"][:, :], ("r0_tmp",))
                    for ci in range(4):
                        c = 4 * wave + ci
                        pps, ppk = pslist[ci]
                        vop(cx, lambda e, c=c, sb0=sb0, pps=pps, hb=hb:
                            e.tensor_add(cx.sb[f"r0{c}"][:, :],
                                         cx.sb[f"xT{c}"][:, hb + sb0:hb + sb0 + 512], pps),
                            r=(f"xT{c}", ppk), w=(f"r0{c}",), tag=f"ra{half}_{blk}_{c}")
                if half == 0 and blk == 0:
                    cx.dbg("r00", cx.sb["r00"][:, :], ("r00",))
                    cx.dbg("yT0", cx.sb["yT0"][:, :], ("yT0",))
                    cx.dbg("yT1", cx.sb["yT1"][:, :], ("yT1",))
                    cx.dbg("yT4", cx.sb["yT4"][:, :], ("yT4",))
                    cx.dbg("yT7", cx.sb["yT7"][:, :], ("yT7",))
                emit_postln(cx, "r0", hb + sb0, 512, (psC, "psC"), (psD, "psD"),
                            (psA0, "psA0"), (psB, "psB"), tag=f"{half}_{blk}")





def emit_l0_ffn(cx):
    pb, nc = cx.pb, cx.nc
    E, HE, HS = 16, 256, 2048

    with ExitStack() as sc:
        scn = cx.sbuf("scn", 128, 16 * E, F32, stack=sc)
        cmb = cx.sbuf("cmb", 128, 16 * E, F32, stack=sc)
        cmbT = cx.sbuf("cmbT", 16, S, BF16, stack=sc)
        cbb = cx.sbuf("cbb", 128, E, F32, stack=sc)
        ones16 = cx.sbuf("ones16", 128, 16, F32, stack=sc)
        oh16 = cx.sbuf("oh16", 16, 16 * 128, BF16, stack=sc)
        rwt = cx.sbuf("rwt", 128, 8 * E, BF16, stack=sc)
        rt = cx.sbuf("rt", 128, 64, F32, stack=sc)
        gs = cx.sbuf("gs", 128, 4, F32, stack=sc)
        hsb = cx.sbuf("hsb", 128, 16 * 256, BF16, stack=sc)
        hall = cx.sbuf("hall", 128, E * 512, BF16, stack=sc)
        get = cx.sbuf("get", 128, 256, BF16, stack=sc)
        hu = cx.sbuf("hu", 128, 256, BF16, stack=sc)
        wsg = cx.sbuf("wsg", 128, 2 * 1024, BF16, stack=sc)
        wgu = cx.sbuf("wgu", 128, 4096, BF16, stack=sc)
        wsd = cx.sbuf("wsd", 128, 2048, BF16, stack=sc)
        dwc = cx.sbuf("dwc", 128, 4096, BF16, stack=sc)
        alloc_ln_bufs(cx, sc, "r1", 256)

        psA0 = cx.psum("psA0", 128, 1024, F32, stack=sc)
        psA1 = cx.psum("psA1", 128, 512, F32, stack=sc)
        psB = cx.psum("psB", 128, 1024, F32, stack=sc)
        psC = cx.psum("psC", 128, 512, F32, stack=sc)
        psD = cx.psum("psD", 128, 512, F32, stack=sc)
        psE = cx.psum("psE", 128, 512, F32, stack=sc)

        x_to_bf(cx)
        pb.op("vector", lambda e: e.memset(ones16[:, :], 1.0), w=("ones16",))
        pb.dma("gpsimd", rwt[:, :],
               bass.AP(cx.dram["l0_rw_t"], 0, [[16, 128], [128 * 16, 8], [1, 16]]),
               w=("rwt",), tag="ldrwt")
        pb.dma("gpsimd", oh16[:, :], cx.dram["c_oh16"][:, :], w=("oh16",), tag="ldoh")
        cbrow = cx.sbuf("cbrow", 1, E, F32, stack=sc)
        pb.dma("sync", cbrow[:, :], cx.dram["l0_cb"][:, :], w=("cbrow",), tag="ldcb")
        top(cx, lambda e: e.matmul(psC[:, 0:E], cx.sb["ones_f"][0:1, :], cbrow[0:1, :],
                                   start=True, stop=True), r=("cbrow", "ones_f"), w=("psC",), tag="cbrep")
        vop(cx, lambda e: e.tensor_copy(cbb[:, :], psC[:, 0:E]), r=("psC",), w=("cbb",), tag="cbcp")

        for sc_i in range(16):
            ps, pk = (psC, "psC") if sc_i % 2 == 0 else (psD, "psD")
            for c in range(NCH):
                top(cx, lambda e, c=c, sc_i=sc_i, ps=ps:
                    e.matmul(ps[:, 0:E], cx.sb[f"xTb{c}"][:, 128 * sc_i:128 * sc_i + 128],
                             rwt[:, E * c:E * c + E], start=(c == 0), stop=(c == NCH - 1)),
                    r=(f"xTb{c}", "rwt"), w=(pk,), tag=f"rl{sc_i}")
            sop(cx, lambda e, sc_i=sc_i, ps=ps:
                e.activation(scn[:, E * sc_i:E * sc_i + E], ps[:, 0:E], AF.Sigmoid),
                r=(pk,), w=("scn",), tag=f"rs{sc_i}")
        for sc_i in range(16):
            V = scn[:, E * sc_i:E * sc_i + E]
            sbb = rt[:, 0:16]
            vop(cx, lambda e, V=V: e.tensor_add(rt[:, 0:16], V, cbb[:, :]),
                r=("scn", "cbb"), w=("rt",), tag=f"r0_{sc_i}")
            for g in range(4):
                sl = rt[:, 4 * g:4 * g + 4]
                vop(cx, lambda e, g=g: e.tensor_reduce(rt[:, 16:17], rt[:, 4 * g:4 * g + 4], AX.X, ALU.max),
                    r=("rt",), w=("rt",), tag=f"r1_{sc_i}_{g}")
                vop(cx, lambda e, g=g: e.tensor_scalar(rt[:, 20:24], rt[:, 4 * g:4 * g + 4],
                                                       rt[:, 16:17], None, ALU.is_ge),
                    r=("rt",), w=("rt",), tag=f"r2_{sc_i}_{g}")
                vop(cx, lambda e, g=g: e.scalar_tensor_tensor(rt[:, 20:24], rt[:, 20:24], -1e30,
                                                              rt[:, 4 * g:4 * g + 4], ALU.mult, ALU.add),
                    r=("rt",), w=("rt",), tag=f"r3_{sc_i}_{g}")
                vop(cx, lambda e, g=g: e.tensor_reduce(rt[:, 17:18], rt[:, 20:24], AX.X, ALU.max),
                    r=("rt",), w=("rt",), tag=f"r4_{sc_i}_{g}")
                vop(cx, lambda e, g=g: e.tensor_add(gs[:, g:g + 1], rt[:, 16:17], rt[:, 17:18]),
                    r=("rt",), w=("gs",), tag=f"r5_{sc_i}_{g}")
            vop(cx, lambda e: e.tensor_reduce(rt[:, 16:17], gs[:, :], AX.X, ALU.max),
                r=("gs",), w=("rt",), tag=f"r6_{sc_i}")
            vop(cx, lambda e: e.tensor_scalar(rt[:, 20:24], gs[:, :], rt[:, 16:17], None, ALU.is_ge),
                r=("gs", "rt"), w=("rt",), tag=f"r7_{sc_i}")
            vop(cx, lambda e: e.scalar_tensor_tensor(rt[:, 20:24], rt[:, 20:24], -1e30, gs[:, :],
                                                     ALU.mult, ALU.add),
                r=("rt", "gs"), w=("rt",), tag=f"r8_{sc_i}")
            vop(cx, lambda e: e.tensor_reduce(rt[:, 17:18], rt[:, 20:24], AX.X, ALU.max),
                r=("rt",), w=("rt",), tag=f"r9_{sc_i}")
            vop(cx, lambda e: e.tensor_scalar(rt[:, 24:28], gs[:, :], rt[:, 17:18], None, ALU.is_ge),
                r=("rt", "gs"), w=("rt",), tag=f"rA_{sc_i}")
            for g in range(4):
                vop(cx, lambda e, g=g: e.tensor_scalar(rt[:, 32 + 4 * g:36 + 4 * g], ones16[:, 0:4],
                                                       rt[:, 24 + g:25 + g], None, ALU.mult),
                    r=("rt", "ones16"), w=("rt",), tag=f"rB_{sc_i}_{g}")
            vop(cx, lambda e: e.tensor_mul(rt[:, 48:64], rt[:, 0:16], rt[:, 32:48]),
                r=("rt",), w=("rt",), tag=f"rC_{sc_i}")
            vop(cx, lambda e: e.tensor_scalar(rt[:, 32:48], rt[:, 32:48], -1.0, None, ALU.add),
                r=("rt",), w=("rt",), tag=f"rD_{sc_i}")
            vop(cx, lambda e: e.scalar_tensor_tensor(rt[:, 48:64], rt[:, 32:48], 1e30, rt[:, 48:64],
                                                     ALU.mult, ALU.add),
                r=("rt",), w=("rt",), tag=f"rE_{sc_i}")
            vop(cx, lambda e: e.tensor_copy(rt[:, 0:16], rt[:, 48:64]),
                r=("rt",), w=("rt",), tag=f"rF_{sc_i}")
            for i in range(4):
                vop(cx, lambda e, i=i: e.tensor_reduce(rt[:, 16 + i:17 + i], rt[:, 0:16], AX.X, ALU.max),
                    r=("rt",), w=("rt",), tag=f"rG_{sc_i}_{i}")
                if i < 3:
                    vop(cx, lambda e, i=i: e.tensor_scalar(rt[:, 32:48], rt[:, 0:16],
                                                           rt[:, 16 + i:17 + i], None, ALU.is_ge),
                        r=("rt",), w=("rt",), tag=f"rH_{sc_i}_{i}")
                    vop(cx, lambda e, i=i: e.scalar_tensor_tensor(rt[:, 0:16], rt[:, 32:48], -1e30,
                                                                  rt[:, 0:16], ALU.mult, ALU.add),
                        r=("rt",), w=("rt",), tag=f"rI_{sc_i}_{i}")
            vop(cx, lambda e: e.tensor_scalar(rt[:, 32:48], rt[:, 48:64], rt[:, 19:20], None, ALU.is_ge),
                r=("rt",), w=("rt",), tag=f"rJ_{sc_i}")
            vop(cx, lambda e, sc_i=sc_i: e.tensor_mul(rt[:, 48:64], rt[:, 32:48],
                                                      scn[:, E * sc_i:E * sc_i + E]),
                r=("rt", "scn"), w=("rt",), tag=f"rK_{sc_i}")
            vop(cx, lambda e: e.tensor_reduce(rt[:, 16:17], rt[:, 48:64], AX.X, ALU.add),
                r=("rt",), w=("rt",), tag=f"rL_{sc_i}")
            vop(cx, lambda e: e.tensor_scalar(rt[:, 16:17], rt[:, 16:17], 1e-20, None, ALU.add),
                r=("rt",), w=("rt",), tag=f"rM_{sc_i}")
            vop(cx, lambda e: e.reciprocal(rt[:, 17:18], rt[:, 16:17]),
                r=("rt",), w=("rt",), tag=f"rN_{sc_i}")
            vop(cx, lambda e, sc_i=sc_i: e.tensor_scalar(cmb[:, E * sc_i:E * sc_i + E], rt[:, 48:64],
                                                         rt[:, 17:18], None, ALU.mult),
                r=("rt",), w=("cmb",), tag=f"rO_{sc_i}")
            top(cx, lambda e, sc_i=sc_i: e.transpose(psE[0:16, 0:128],
                                                     cmb[:, E * sc_i:E * sc_i + E], cx.sb["identf"][:, :]),
                r=("cmb", "identf"), w=("psE",), tag=f"rT_{sc_i}")
            vop(cx, lambda e, sc_i=sc_i: e.tensor_copy(cmbT[:, 128 * sc_i:128 * sc_i + 128],
                                                       psE[0:16, 0:128]),
                r=("psE",), w=("cmbT",), tag=f"rU_{sc_i}")

        if cx.cfg.get("debug"):
            cx.dbg("scn", scn[:, :], ("scn",))
            cx.dbg("cmb", cmb[:, :], ("cmb",))
        for blk in range(8):
            sb0 = 256 * blk
            NB = 256
            for m in range(16):
                wslot = m % 2
                pb.dma("gpsimd", wsg[:, 1024 * wslot:1024 * wslot + 1024],
                       bass.AP(cx.dram["l0_sgup_t"], 128 * m,
                               [[HS, 128], [128 * HS, 8], [1, 128]]),
                       w=(f"W:wsg{wslot}",), tag=f"ldsg{blk}_{m}")
                psg, psgk = (psA0, "psA0") if m % 2 == 0 else (psA1, "psA1")
                for kc in range(NCH):
                    top(cx, lambda e, kc=kc, psg=psg, wslot=wslot, sb0=sb0:
                        e.matmul(psg[:, 0:NB], wsg[:, 1024 * wslot + 128 * kc:1024 * wslot + 128 * kc + 128],
                                 cx.sb[f"xTb{kc}"][:, sb0:sb0 + NB],
                                 start=(kc == 0), stop=(kc == NCH - 1)),
                        r=(f"W:wsg{wslot}", f"xTb{kc}"), w=(psgk,), tag=f"sg{blk}_{m}")
                sop(cx, lambda e, psg=psg: e.activation(get[:, 0:NB], psg[:, 0:NB], AF.Silu),
                    r=(psgk,), w=("get",), tag=f"sgs{blk}_{m}")
                psu, psuk = (psB, "psB") if m % 2 == 0 else (psC, "psC")
                pb.dma("gpsimd", wsg[:, 1024 * wslot:1024 * wslot + 1024],
                       bass.AP(cx.dram["l0_su_t"], 128 * m,
                               [[HS, 128], [128 * HS, 8], [1, 128]]),
                       w=(f"W:wsg{wslot}",), tag=f"ldsu{blk}_{m}")
                for kc in range(NCH):
                    top(cx, lambda e, kc=kc, psu=psu, wslot=wslot, sb0=sb0:
                        e.matmul(psu[:, 0:NB], wsg[:, 1024 * wslot + 128 * kc:1024 * wslot + 128 * kc + 128],
                                 cx.sb[f"xTb{kc}"][:, sb0:sb0 + NB],
                                 start=(kc == 0), stop=(kc == NCH - 1)),
                        r=(f"W:wsg{wslot}", f"xTb{kc}"), w=(psuk,), tag=f"su{blk}_{m}")
                vop(cx, lambda e, m=m, psu=psu: e.tensor_mul(hsb[:, NB * m:NB * m + NB],
                                                             get[:, 0:NB], psu[:, 0:NB]),
                    r=("get", psuk), w=("hsb",), tag=f"sh{blk}_{m}")
            for ei in range(E):
                pb.dma("gpsimd", wgu[:, 0:2048],
                       bass.AP(cx.dram["l0_gw"], ei * 1024 * HE,
                               [[HE, 128], [128 * HE, 8], [1, HE]]),
                       w=("W:wgu",), tag=f"ldg{blk}_{ei}")
                pb.dma("gpsimd", wgu[:, 2048:4096],
                       bass.AP(cx.dram["l0_uw"], ei * 1024 * HE,
                               [[HE, 128], [128 * HE, 8], [1, HE]]),
                       w=("W:wgu",), tag=f"ldu{blk}_{ei}")
                top(cx, lambda e, ei=ei, sb0=sb0:
                    e.matmul(psE[:, 0:NB], oh16[:, 128 * ei:128 * ei + 128],
                             cmbT[:, sb0:sb0 + NB], start=True, stop=True),
                    r=("oh16", "cmbT"), w=("psE",), tag=f"cb{blk}_{ei}")
                for hm in range(2):
                    psg, psgk = (psC, "psC") if hm == 0 else (psD, "psD")
                    psu, psuk = (psA1, "psA1") if hm == 0 else (psA0, "psA0")
                    for kc in range(NCH):
                        top(cx, lambda e, kc=kc, hm=hm, psg=psg, sb0=sb0:
                            e.matmul(psg[:, 0:NB], wgu[:, 256 * kc + 128 * hm:256 * kc + 128 * hm + 128],
                                     cx.sb[f"xTb{kc}"][:, sb0:sb0 + NB],
                                     start=(kc == 0), stop=(kc == NCH - 1)),
                            r=("W:wgu", f"xTb{kc}"), w=(psgk,), tag=f"eg{blk}_{ei}_{hm}")
                        top(cx, lambda e, kc=kc, hm=hm, psu=psu, sb0=sb0:
                            e.matmul(psu[:, 0:NB], wgu[:, 2048 + 256 * kc + 128 * hm:2048 + 256 * kc + 128 * hm + 128],
                                     cx.sb[f"xTb{kc}"][:, sb0:sb0 + NB],
                                     start=(kc == 0), stop=(kc == NCH - 1)),
                            r=("W:wgu", f"xTb{kc}"), w=(psuk,), tag=f"eu{blk}_{ei}_{hm}")
                    sop(cx, lambda e, psg=psg: e.activation(get[:, 0:NB], psg[:, 0:NB], AF.Silu),
                        r=(psgk,), w=("get",), tag=f"es{blk}_{ei}_{hm}")
                    vop(cx, lambda e, psu=psu: e.tensor_mul(hu[:, 0:NB], get[:, 0:NB], psu[:, 0:NB]),
                        r=("get", psuk), w=("hu",), tag=f"eh{blk}_{ei}_{hm}")
                    vop(cx, lambda e, ei=ei, hm=hm: e.tensor_mul(
                            hall[:, NB * (2 * ei + hm):NB * (2 * ei + hm) + NB],
                            hu[:, 0:NB], psE[:, 0:NB]),
                        r=("hu", "psE"), w=("hall",), tag=f"ec{blk}_{ei}_{hm}")
            banks = [(psA0[:, 0:NB], "psA0"), (psA0[:, 512:512 + NB], "psA0"),
                     (psA1[:, 0:NB], "psA1"), (psB[:, 0:NB], "psB"),
                     (psB[:, 512:512 + NB], "psB"), (psC[:, 0:NB], "psC"),
                     (psD[:, 0:NB], "psD"), (psE[:, 0:NB], "psE")]
            for c in range(NCH):
                pps, ppk = banks[c]
                pb.dma("gpsimd", wsd[:, :],
                       bass.AP(cx.dram["l0_sdw_t"], 128 * c, [[1024, 128], [128 * 1024, 16], [1, 128]]),
                       w=(f"W:wsd{c % 1}",), tag=f"ldsd{blk}_{c}") if False else None
            for c in range(NCH):
                pps, ppk = banks[c]
                wslot = c % 2
                pb.dma("gpsimd", wsd[:, :],
                       bass.AP(cx.dram["l0_sdw_t"], 128 * c, [[1024, 128], [128 * 1024, 16], [1, 128]]),
                       w=("W:wsd",), tag=f"ldsd{blk}_{c}")
                for kc in range(16):
                    top(cx, lambda e, kc=kc, pps=pps:
                        e.matmul(pps, wsd[:, 128 * kc:128 * kc + 128],
                                 hsb[:, NB * kc:NB * kc + NB],
                                 start=(kc == 0), stop=False),
                        r=("W:wsd", "hsb"), w=(ppk,), tag=f"ds{blk}_{c}_{kc}")
                pb.dma("gpsimd", dwc[:, :],
                       bass.AP(cx.dram["l0_dw"], 128 * c, [[1024, 128], [128 * 1024, 32], [1, 128]]),
                       w=("W:dwc",), tag=f"lddw{blk}_{c}")
                for st_i in range(32):
                    top(cx, lambda e, st_i=st_i, pps=pps:
                        e.matmul(pps, dwc[:, 128 * st_i:128 * st_i + 128],
                                 hall[:, NB * st_i:NB * st_i + NB],
                                 start=False, stop=(st_i == 31)),
                        r=("W:dwc", "hall"), w=(ppk,), tag=f"dr{blk}_{c}_{st_i}")
            for c in range(NCH):
                pps, ppk = banks[c]
                vop(cx, lambda e, c=c, sb0=sb0, pps=pps:
                    e.tensor_add(cx.sb[f"r1{c}"][:, 0:NB], cx.sb[f"xT{c}"][:, sb0:sb0 + NB], pps),
                    r=(f"xT{c}", ppk), w=(f"r1{c}",), tag=f"dra{blk}_{c}")
            emit_postln(cx, "r1", sb0, NB, (psC, "psC"), (psD, "psD"),
                        (psA0, "psA0"), (psB, "psB"), tag=f"f{blk}")



def attn_head_block(cx, B, qbuf, qkey, kget, kkey, vlist, vkey, nlist, blk_j0,
                    ybuf, ykey, hh, scl, tagp, W=512):
    pb = cx.pb
    psA0, psA1, psB, psE = B["psA0"], B["psA1"], B["psB"], B["psE"]
    pt, yden, ytmp, trimask = B["pt"], B["yden"], B["ytmp"], B["trimask"]
    hp = 64 * (hh % 2)
    hc = hh // 2
    nn = len(nlist)
    for ni, n in enumerate(nlist):
        pa, pak = (psA0, "psA0") if ni % 2 == 0 else (psA1, "psA1")
        top(cx, lambda e, n=n, pa=pa, hp=hp, hc=hc:
            e.matmul(pa[:, 0:W], kget(n), qbuf[hp:hp + 64, W * hc:W * hc + W],
                     start=True, stop=True),
            r=(kkey, qkey), w=(pak,), tag=f"{tagp}s{ni}")
        j = n - blk_j0
        if 0 <= j <= 3:
            vop(cx, lambda e, j=j, pa=pa: e.tensor_add(pa[:, 128 * j:128 * j + 128],
                                                       pa[:, 128 * j:128 * j + 128], trimask[:, :]),
                r=(pak, "trimask"), w=(pak,), tag=f"{tagp}m{ni}")
            if j > 0:
                vop(cx, lambda e, ni=ni, j=j: e.memset(pt[:, 512 * (ni % 2):512 * (ni % 2) + 128 * j], 0.0),
                    w=(f"pt{ni % 2}",), tag=f"{tagp}z{ni}")
            sop(cx, lambda e, ni=ni, j=j, pa=pa:
                e.activation(pt[:, 512 * (ni % 2) + 128 * j:512 * (ni % 2) + W],
                             pa[:, 128 * j:W], AF.Exp, scale=scl),
                r=(pak,), w=(f"pt{ni % 2}",), wd=512, tag=f"{tagp}eb{ni}")
        else:
            sop(cx, lambda e, ni=ni, pa=pa:
                e.activation(pt[:, 512 * (ni % 2):512 * (ni % 2) + W],
                             pa[:, 0:W], AF.Exp, scale=scl),
                r=(pak,), w=(f"pt{ni % 2}",), wd=512, tag=f"{tagp}e{ni}")
        top(cx, lambda e, ni=ni, n=n, hh=hh:
            e.matmul(psB[0:65, 0:W], vlist[n][:, 65 * hh:65 * hh + 65],
                     pt[:, 512 * (ni % 2):512 * (ni % 2) + W],
                     start=(ni == 0), stop=(ni == nn - 1)),
            r=(f"pt{ni % 2}", vkey + str(n)), w=("psB",), tag=f"{tagp}p{ni}")
    vop(cx, lambda e: e.reciprocal(yden[0:1, 0:W], psB[64:65, 0:W]),
        r=("psB",), w=("yden",), tag=f"{tagp}rc")
    top(cx, lambda e: e.matmul(psE[0:64, 0:W], cx.sb["ones_f"][0:1, 0:64],
                               yden[0:1, 0:W], start=True, stop=True),
        r=("yden", "ones_f"), w=("psE",), tag=f"{tagp}rd")
    vop(cx, lambda e: e.tensor_copy(ytmp[:, 0:W], psB[0:64, 0:W]),
        r=("psB",), w=("ytmp",), wd=512, tag=f"{tagp}yc")
    vop(cx, lambda e, hp=hp, hc=hc:
        e.tensor_mul(ybuf[hp:hp + 64, W * hc:W * hc + W], ytmp[:, 0:W], psE[0:64, 0:W]),
        r=("ytmp", "psE"), w=(ykey,), wd=512, tag=f"{tagp}yn")





def emit_l2_attn(cx):
    pb, nc = cx.pb, cx.nc
    import math as _m
    H = 8
    WT = 160
    Q_SCALE = (1.0 / _m.sqrt(128.0)) / _m.log(2.0)
    K_SCALE = _m.log(1.0 + _m.e) / _m.log(2.0)

    with ExitStack() as sc:
        x_to_bf(cx)
        pdsc = cx.sbuf("pdsc", 128, 2, F32, stack=sc)
        pb.dma("sync", pdsc[:, 0:1], cx.dram["l2_pds"][:, :], w=("pds",), tag="ldpds")
        sop(cx, lambda e: e.activation(pdsc[:, 1:2], pdsc[:, 0:1], AF.Exp),
            r=("pds",), w=("pds1",), tag="pds1")
        sop(cx, lambda e: e.activation(pdsc[:, 1:2], pdsc[:, 1:2], AF.Ln, bias=1.0),
            r=("pds1",), w=("pds2",), tag="pds2")
        vop(cx, lambda e: e.tensor_scalar(pdsc[:, 0:1], pdsc[:, 1:2], Q_SCALE, None, ALU.mult),
            r=("pds2",), w=("pdsc",), tag="pds3")
        maskb = cx.sbuf("maskb", 128, 16 * WT, BF16, stack=sc)
        pb.dma("gpsimd", maskb[:, :], cx.dram["l2_mask"][:, :], w=("maskb",), tag="ldmask")
        onesbc = cx.sbuf("onesbc", 128, 8, BF16, stack=sc)
        pb.op("vector", lambda e: e.memset(onesbc[:, :], 1.0), w=("onesbc",))
        zrow = cx.sbuf("zrow", 1, 2660, F32, stack=sc)
        pb.op("vector", lambda e: e.memset(zrow[:, :], 0.0), w=("zrow",))
        zrowb = cx.sbuf("zrowb", 1, 2660, BF16, stack=sc)
        pb.op("vector", lambda e: e.memset(zrowb[:, :], 0.0), w=("zrowb",))

        psP0 = cx.psum("psP0", 128, 512, F32, stack=sc)
        psP1 = cx.psum("psP1", 128, 512, F32, stack=sc)
        psS0 = cx.psum("psS0", 128, 512, F32, stack=sc)
        psS1 = cx.psum("psS1", 128, 512, F32, stack=sc)
        psT = cx.psum("psT", 128, 512, F32, stack=sc)
        psO = cx.psum("psO", 128, 512, F32, stack=sc)
        psDen = cx.psum("psDen", 128, 512, F32, stack=sc)
        psEb = cx.psum("psEb", 128, 512, F32, stack=sc)
        psPP = [(psP0, "psP0"), (psP1, "psP1")]
        psSS = [(psS0, "psS0"), (psS1, "psS1")]
        psTb = psT.ap().bitcast(BF16)

        for slot in range(2):
            for g0 in (slot * 2080, slot * 2080 + 2064):
                pb.dma("sync", bass.AP(cx.dram["scrRp"], g0 * 25, [[400, 1], [1, 400]]),
                       zrow[0:1, 0:400], r=("zrow",), w=(f"scrRp{slot}",), tag=f"zg{slot}_{g0}")
        for h in range(H):
            pb.dma("sync", bass.AP(cx.dram["scrV"], (2080 * h) * 128, [[128, 12], [1, 128]]),
                   zrowb[0:1, 0:12 * 128], r=("zrowb",), w=("scrV",), tag=f"zv0_{h}")
            pb.dma("sync", bass.AP(cx.dram["scrV"], (2080 * h + 2060) * 128, [[128, 20], [1, 128]]),
                   zrowb[0:1, 0:20 * 128], r=("zrowb",), w=("scrV",), tag=f"zv1_{h}")

        relkT = cx.sbuf("relkT", 128, 8 * 25, BF16, stack=sc)
        with ExitStack() as s0:
            wrelb = cx.sbuf("wrelb", 128, 8 * 1024, BF16, stack=s0)
            posTb = cx.sbuf("posTb", 128, 8 * 25, BF16, stack=s0)
            pb.dma("gpsimd", wrelb[:, :],
                   bass.AP(cx.dram["l2_wrel"], 0, [[1024, 128], [128 * 1024, 8], [1, 1024]]),
                   w=("W:wrelb",), tag="ldwrel")
            pb.dma("gpsimd", posTb[:, :],
                   bass.AP(cx.dram["l2_posT"], 0, [[25, 128], [128 * 25, 8], [1, 25]]),
                   w=("posTb",), tag="ldposT")
            for hd in range(8):
                for c in range(NCH):
                    top(cx, lambda e, hd=hd, c=c:
                        e.matmul(psP0[:, 0:25], wrelb[:, 1024 * c + 128 * hd:1024 * c + 128 * hd + 128],
                                 posTb[:, 25 * c:25 * c + 25], start=(c == 0), stop=(c == NCH - 1)),
                        r=("W:wrelb", "posTb"), w=("psP0",), tag=f"rk{hd}")
                vop(cx, lambda e, hd=hd: e.tensor_copy(relkT[:, 25 * hd:25 * hd + 25], psP0[:, 0:25]),
                    r=("psP0",), w=("relkT",), tag=f"rke{hd}")
        pb.barrier()

        with ExitStack() as s1:
            wvb = cx.sbuf("wvb", 128, 8 * 1024, BF16, stack=s1)
            vstage = cx.sbuf("vstage", 128, 1024, BF16, stack=s1)
            pb.dma("gpsimd", wvb[:, :],
                   bass.AP(cx.dram["l2_wv"], 0, [[1024, 128], [128 * 1024, 8], [1, 1024]]),
                   w=("W:wvb",), tag="ldwv")
            for st_ in range(16):
                for half in range(2):
                    ps, pk = psPP[half]
                    for c in range(NCH):
                        top(cx, lambda e, st_=st_, half=half, c=c, ps=ps:
                            e.matmul(ps[:, 0:512], cx.sb[f"xTb{c}"][:, 128 * st_:128 * st_ + 128],
                                     wvb[:, 1024 * c + 512 * half:1024 * c + 512 * half + 512],
                                     start=(c == 0), stop=(c == NCH - 1)),
                            r=("W:wvb", f"xTb{c}"), w=(pk,), tag=f"v{st_}_{half}")
                    vop(cx, lambda e, half=half, ps=ps:
                        e.tensor_copy(vstage[:, 512 * half:512 * half + 512], ps[:, 0:512]),
                        r=(pk,), w=("vstage",), wd=512, tag=f"ve{st_}_{half}")
                for h in range(H):
                    pb.dma("sync",
                           bass.AP(cx.dram["scrV"], (2080 * h + 12 + 128 * st_) * 128,
                                   [[128, 128], [1, 128]]),
                           vstage[:, 128 * h:128 * h + 128],
                           r=("vstage",), w=("scrV",), tag=f"vs{st_}_{h}")
        pb.barrier()

        yTl2 = [cx.sbuf(f"yl2_{h}", 128, S, BF16, stack=sc) for h in range(H)]
        sh = ExitStack()
        qTh = [cx.sbuf(f"l2q{sl}", 128, S, BF16, stack=sh) for sl in range(2)]
        kTx = [cx.sbuf(f"l2k{sl}", 128, 2080, BF16, stack=sh) for sl in range(2)]
        for sl in range(2):
            pb.op("vector", lambda e, sl=sl: e.memset(kTx[sl][:, 0:12], 0.0), w=(f"kl2{sl}",))
            pb.op("vector", lambda e, sl=sl: e.memset(kTx[sl][:, 2060:2080], 0.0), w=(f"kl2{sl}",))
        wtr = [cx.sbuf(f"wtr{sl}", 128, 2 * 1024, BF16, stack=sh) for sl in range(2)]
        rpE = cx.sbuf("rpE", 128, 512, F32, stack=sh)
        rpT = cx.sbuf("rpT", 128, 32, F32, stack=sh)
        Dt = [cx.sbuf(f"Dt{ms}", 128, WT, F32, stack=sh) for ms in range(2)]
        Xb = [cx.sbuf(f"Xb{ms}", 128, WT, F32, stack=sh) for ms in range(2)]
        Yb = cx.sbuf("Yb", 128, WT, F32, stack=sh)
        ptb = cx.sbuf("ptb", 128, WT, BF16, stack=sh)
        pt2 = [cx.sbuf(f"pt2_{ms}", 128, WT, BF16, stack=sh) for ms in range(2)]
        ptT = [cx.sbuf(f"ptT{ms}", 128, 256, BF16, stack=sh) for ms in range(2)]
        vsl = [cx.sbuf(f"vsl{ms}", 128, 256, BF16, stack=sh) for ms in range(2)]
        ytmp2 = cx.sbuf("ytmp2", 128, 128, F32, stack=sh)
        yden2 = cx.sbuf("yden2", 1, 128, F32, stack=sh)

        for h in range(H):
            sl = h % 2
            qk, kk = f"ql2{sl}", f"kl2{sl}"
            for j, wnm in enumerate(("l2_wq", "l2_wk")):
                pb.dma("gpsimd", bass.AP(wtr[sl], 1024 * j, [[2 * 1024, 128], [128, 8], [1, 128]]),
                       bass.AP(cx.dram[wnm], 128 * h, [[1024, 128], [128 * 1024, 8], [1, 128]]),
                       w=(f"W:wtr{sl}",), tag=f"lw{h}_{j}")
            for qb_ in range(4):
                ps, pk = psPP[qb_ % 2]
                for c in range(NCH):
                    top(cx, lambda e, qb_=qb_, c=c, ps=ps, sl=sl:
                        e.matmul(ps[:, 0:512], wtr[sl][:, 128 * c:128 * c + 128],
                                 cx.sb[f"xTb{c}"][:, 512 * qb_:512 * qb_ + 512],
                                 start=(c == 0), stop=(c == NCH - 1)),
                        r=(f"W:wtr{sl}", f"xTb{c}"), w=(pk,), tag=f"qp{h}_{qb_}")
                vop(cx, lambda e, qb_=qb_, ps=ps, sl=sl:
                    e.tensor_scalar(qTh[sl][:, 512 * qb_:512 * qb_ + 512], ps[:, 0:512],
                                    pdsc[:, 0:1], None, ALU.mult),
                    r=(pk, "pdsc"), w=(qk,), wd=512, tag=f"qe{h}_{qb_}")
            for qb_ in range(4):
                top(cx, lambda e, qb_=qb_, h=h, sl=sl:
                    e.matmul(psP0[0:25, 0:512], relkT[:, 25 * h:25 * h + 25],
                             qTh[sl][:, 512 * qb_:512 * qb_ + 512], start=True, stop=True),
                    r=("relkT", qk), w=("psP0",), tag=f"rp{h}_{qb_}")
                vop(cx, lambda e, qb_=qb_: e.tensor_copy(rpE[0:25, 0:512], psP0[0:25, 0:512]),
                    r=("psP0",), w=("rpE",), wd=512, tag=f"rpe{h}_{qb_}")
                for stl in range(4):
                    top(cx, lambda e, stl=stl:
                        e.transpose(psP1[:, 0:25], rpE[0:25, 128 * stl:128 * stl + 128],
                                    cx.sb["identf"][0:25, 0:25]),
                        r=("rpE", "identf"), w=("psP1",), tag=f"rpt{h}_{qb_}_{stl}")
                    vop(cx, lambda e: e.tensor_copy(rpT[:, 0:25], psP1[:, 0:25]),
                        r=("psP1",), w=("rpT",), tag=f"rpte{h}_{qb_}_{stl}")
                    pb.dma("sync",
                           bass.AP(cx.dram["scrRp"],
                                   (2080 * sl + 16 + 512 * qb_ + 128 * stl) * 25,
                                   [[25, 128], [1, 25]]),
                           rpT[:, 0:25], r=("rpT",), w=(f"scrRp{sl}",), tag=f"rps{h}_{qb_}_{stl}")
            for qb_ in range(4):
                ps, pk = psPP[qb_ % 2]
                for c in range(NCH):
                    top(cx, lambda e, qb_=qb_, c=c, ps=ps, sl=sl:
                        e.matmul(ps[:, 0:512], wtr[sl][:, 1024 + 128 * c:1024 + 128 * c + 128],
                                 cx.sb[f"xTb{c}"][:, 512 * qb_:512 * qb_ + 512],
                                 start=(c == 0), stop=(c == NCH - 1)),
                        r=(f"W:wtr{sl}", f"xTb{c}"), w=(pk,), tag=f"kp{h}_{qb_}")
                vop(cx, lambda e, qb_=qb_, ps=ps, sl=sl:
                    e.tensor_scalar(kTx[sl][:, 12 + 512 * qb_:12 + 512 * qb_ + 512], ps[:, 0:512],
                                    K_SCALE, None, ALU.mult),
                    r=(pk,), w=(kk,), wd=512, tag=f"ke{h}_{qb_}")
            if cx.cfg.get("debug") and h == 0:
                cx.dbg("l2qT0", qTh[0][:, :], ("ql20",))
                cx.dbg("l2kT0", kTx[0][:, :], ("kl20",))
                cx.dbg("l2relkT", relkT[:, :], ("relkT",))
                cx.dbg("l2pdsc", pdsc[:, :], ("pdsc",))
            for mm_ in range(16):
                ms = mm_ % 2
                psS, psSk = psSS[ms]
                s0_ = 128 * mm_
                t0p = 12 * (s0_ // 12)
                b0 = t0p - s0_
                tg = f"a{h}_{mm_}"
                pb.dma("scalar", Dt[ms][:, :],
                       bass.AP(cx.dram["scrRp"], (2080 * sl + 16 + s0_) * 25 + b0,
                               [[24, 128], [1, WT]]),
                       r=(f"scrRp{sl}",), w=(f"Dt{ms}",), tag=f"{tg}d")
                pb.dma("scalar", vsl[ms][:, 0:128],
                       bass.AP(cx.dram["scrV"], (2080 * h + t0p) * 128, [[128, 128], [1, 128]]),
                       r=("scrV",), w=(f"vsl{ms}",), tag=f"{tg}va")
                pb.dma("scalar", vsl[ms][0:32, 128:256],
                       bass.AP(cx.dram["scrV"], (2080 * h + t0p + 128) * 128, [[128, 32], [1, 128]]),
                       r=("scrV",), w=(f"vsl{ms}",), tag=f"{tg}vb")
                top(cx, lambda e, psS=psS, sl=sl, s0_=s0_, t0p=t0p:
                    e.matmul(psS[:, 0:WT], qTh[sl][:, s0_:s0_ + 128],
                             kTx[sl][:, t0p:t0p + WT], start=True, stop=True),
                    r=(qk, kk), w=(psSk,), tag=f"{tg}s")
                vop(cx, lambda e, psS=psS, ms=ms: e.tensor_add(Xb[ms][:, :], psS[:, 0:WT],
                                                               Dt[ms][:, :]),
                    r=(psSk, f"Dt{ms}"), w=(f"Xb{ms}",), tag=f"{tg}x")
                sop(cx, lambda e, ms=ms: e.activation(Yb[:, :], Xb[ms][:, :], AF.Tanh, scale=0.02),
                    r=(f"Xb{ms}",), w=("Yb",), tag=f"{tg}t")
                sop(cx, lambda e: e.activation(ptb[:, :], Yb[:, :], AF.Exp, scale=50.0),
                    r=("Yb",), w=("ptb",), tag=f"{tg}e")
                vop(cx, lambda e, ms=ms, mm_=mm_: e.tensor_mul(pt2[ms][:, :], ptb[:, :],
                                                               maskb[:, WT * mm_:WT * mm_ + WT]),
                    r=("ptb", "maskb"), w=(f"pt2_{ms}",), tag=f"{tg}m")
                top(cx, lambda e, ms=ms: e.transpose(psTb[:, 0:128], pt2[ms][:, 0:128],
                                                     cx.sb["ident"][:, :]),
                    r=(f"pt2_{ms}", "ident"), w=("psT",), tag=f"{tg}t1")
                top(cx, lambda e, ms=ms: e.transpose(psTb[0:32, 128:256], pt2[ms][:, 128:WT],
                                                     cx.sb["ident"][:, :]),
                    r=(f"pt2_{ms}", "ident"), w=("psT",), tag=f"{tg}t2")
                vop(cx, lambda e, ms=ms: e.tensor_copy(ptT[ms][:, 0:128], psTb[:, 0:128]),
                    r=("psT",), w=(f"ptT{ms}",), tag=f"{tg}pe1")
                vop(cx, lambda e, ms=ms: e.tensor_copy(ptT[ms][0:32, 128:256], psTb[0:32, 128:256]),
                    r=("psT",), w=(f"ptT{ms}",), tag=f"{tg}pe2")
                top(cx, lambda e, ms=ms: e.matmul(psO[:, 0:128], vsl[ms][:, 0:128],
                                                  ptT[ms][:, 0:128], start=True, stop=False),
                    r=(f"vsl{ms}", f"ptT{ms}"), w=("psO",), tag=f"{tg}p1")
                top(cx, lambda e, ms=ms: e.matmul(psO[:, 0:128], vsl[ms][0:32, 128:256],
                                                  ptT[ms][0:32, 128:256], start=False, stop=True),
                    r=(f"vsl{ms}", f"ptT{ms}"), w=("psO",), tag=f"{tg}p2")
                top(cx, lambda e, ms=ms: e.matmul(psDen[0:1, 0:128], onesbc[:, 0:1],
                                                  ptT[ms][:, 0:128], start=True, stop=False),
                    r=(f"ptT{ms}", "onesbc"), w=("psDen",), tag=f"{tg}d1")
                top(cx, lambda e, ms=ms: e.matmul(psDen[0:1, 0:128], onesbc[0:32, 0:1],
                                                  ptT[ms][0:32, 128:256], start=False, stop=True),
                    r=(f"ptT{ms}", "onesbc"), w=("psDen",), tag=f"{tg}d2")
                vop(cx, lambda e: e.reciprocal(yden2[0:1, 0:128], psDen[0:1, 0:128]),
                    r=("psDen",), w=("yden2",), tag=f"{tg}rc")
                top(cx, lambda e: e.matmul(psEb[:, 0:128], cx.sb["ones_f"][0:1, :],
                                           yden2[0:1, 0:128], start=True, stop=True),
                    r=("yden2", "ones_f"), w=("psEb",), tag=f"{tg}rd")
                vop(cx, lambda e: e.tensor_copy(ytmp2[:, :], psO[:, 0:128]),
                    r=("psO",), w=("ytmp2",), tag=f"{tg}yc")
                vop(cx, lambda e, h=h, s0_=s0_: e.tensor_mul(yTl2[h][:, s0_:s0_ + 128],
                                                             ytmp2[:, :], psEb[:, 0:128]),
                    r=("ytmp2", "psEb"), w=(f"yl2_{h}",), tag=f"{tg}yn")
                if cx.cfg.get("debug") and h == 0 and mm_ == 1:
                    cx.dbg("l2Dt1", Dt[1][:, :], ("Dt1",))
                    cx.dbg("l2Xb1", Xb[1][:, :], ("Xb1",))
                    cx.dbg("l2pt21", pt2[1][:, :], ("pt2_1",))
                    cx.dbg("l2vsl1", vsl[1][:, :], ("vsl1",))
                if cx.cfg.get("debug") and h == 0 and mm_ == 15:
                    cx.dbg("l2y0", yTl2[0][:, :], ("yl2_0",))
        sh.close()
        pb.barrier()
        with ExitStack() as s2:
            wpb2 = cx.sbuf("wpb2", 128, 8 * 512, BF16, stack=s2)
            alloc_ln_bufs(cx, s2, "r5", 512)
            for tq in range(4):
                for h_ in range(2):
                    pb.dma("gpsimd", wpb2[:, :],
                           bass.AP(cx.dram["l2_wpost"], 512 * h_,
                                   [[1024, 128], [128 * 1024, 8], [1, 512]]),
                           w=("W:wpb2",), tag=f"lpo{tq}_{h_}")
                    for ei in range(4):
                        ec = 4 * h_ + ei
                        ps, pk = (psP0, "psP0") if ec % 2 == 0 else (psP1, "psP1")
                        for hc in range(8):
                            top(cx, lambda e, ei=ei, hc=hc, ps=ps, tq=tq:
                                e.matmul(ps[:, 0:512], wpb2[:, 512 * hc + 128 * ei:512 * hc + 128 * ei + 128],
                                         yTl2[hc][:, 512 * tq:512 * tq + 512],
                                         start=(hc == 0), stop=(hc == 7)),
                                r=("W:wpb2", f"yl2_{hc}"), w=(pk,), tag=f"po{tq}_{ec}")
                        vop(cx, lambda e, ec=ec, ps=ps, tq=tq:
                            e.tensor_add(cx.sb[f"r5{ec}"][:, 0:512], ps[:, 0:512],
                                         cx.sb[f"xT{ec}"][:, 512 * tq:512 * tq + 512]),
                            r=(pk, f"xT{ec}"), w=(f"r5{ec}",), wd=512, tag=f"pr{tq}_{ec}")
                emit_postln(cx, "r5", 512 * tq, 512, (psS0, "psS0"), (psS1, "psS1"),
                            (psO, "psO"), (psEb, "psEb"), tag=f"l2p{tq}")




def emit_l2_ffn(cx):
    pb, nc = cx.pb, cx.nc
    NI = 11

    with ExitStack() as sc:
        x_to_bf(cx)
        psG = cx.psum("psG", 128, 512, F32, stack=sc)
        psU = cx.psum("psU", 128, 512, F32, stack=sc)
        psD0 = cx.psum("psD0", 128, 512, F32, stack=sc)
        psD1 = cx.psum("psD1", 128, 512, F32, stack=sc)
        psB = cx.psum("psB", 128, 512, F32, stack=sc)
        psL = cx.psum("psL", 128, 512, F32, stack=sc)
        psDD = [(psD0, "psD0"), (psD1, "psD1")]

        fc = [cx.sbuf(f"fcb{c}", 128, S, BF16, stack=sc) for c in range(NCH)]

        with ExitStack() as sr:
            gtb = cx.sbuf("gtb", 128, 64, BF16, stack=sr)
            sgw = cx.sbuf("sgw", 128, 8, BF16, stack=sr)
            lgE = cx.sbuf("lgE", 128, S, F32, stack=sr)
            sgrow = cx.sbuf("sgrow", 1, S, F32, stack=sr)
            cmbv = cx.sbuf("cmbv", 128, 128, F32, stack=sr)
            cmbT2 = cx.sbuf("cmbT2", 128, 128, F32, stack=sr)
            rt2 = cx.sbuf("rt2", 128, 48, F32, stack=sr)
            pb.dma("gpsimd", gtb[:, :],
                   bass.AP(cx.dram["l2f_gt"], 0, [[8, 128], [128 * 8, 8], [1, 8]]),
                   w=("gtb",), tag="ldgt")
            pb.dma("gpsimd", sgw[:, 0:8],
                   bass.AP(cx.dram["l2f_sgw"], 0, [[1, 128], [128, 8]]),
                   w=("sgw",), tag="ldsgw", allow_slow_non_contiguous=True)
            for tb in range(4):
                for c in range(NCH):
                    top(cx, lambda e, tb=tb, c=c:
                        e.matmul(psL[0:8, 0:512], gtb[:, 8 * c:8 * c + 8],
                                 cx.sb[f"xTb{c}"][:, 512 * tb:512 * tb + 512],
                                 start=(c == 0), stop=(c == NCH - 1)),
                        r=("gtb", f"xTb{c}"), w=("psL",), tag=f"lg{tb}")
                vop(cx, lambda e, tb=tb: e.tensor_copy(lgE[0:8, 512 * tb:512 * tb + 512],
                                                       psL[0:8, 0:512]),
                    r=("psL",), w=("lgE",), wd=512, tag=f"lge{tb}")
                for c in range(NCH):
                    top(cx, lambda e, tb=tb, c=c:
                        e.matmul(psB[0:1, 0:512], sgw[:, c:c + 1],
                                 cx.sb[f"xTb{c}"][:, 512 * tb:512 * tb + 512],
                                 start=(c == 0), stop=(c == NCH - 1)),
                        r=("sgw", f"xTb{c}"), w=("psB",), tag=f"sg{tb}")
                sop(cx, lambda e, tb=tb: e.activation(sgrow[0:1, 512 * tb:512 * tb + 512],
                                                      psB[0:1, 0:512], AF.Sigmoid),
                    r=("psB",), w=("sgrow",), tag=f"sge{tb}")
            pb.dma("sync", bass.AP(cx.dram["scrC"], 0, [[1, 1], [1, S]]), sgrow[0:1, :],
                   r=("sgrow",), w=("scrC",), tag="sgst")
            for tt in range(16):
                top(cx, lambda e, tt=tt: e.transpose(psB[:, 0:8], lgE[0:8, 128 * tt:128 * tt + 128],
                                                     cx.sb["identf"][0:8, 0:8]),
                    r=("lgE", "identf"), w=("psB",), tag=f"rt{tt}")
                vop(cx, lambda e: e.tensor_copy(rt2[:, 0:8], psB[:, 0:8]),
                    r=("psB",), w=("rt2",), tag=f"rc{tt}")
                vop(cx, lambda e: e.tensor_reduce(rt2[:, 8:9], rt2[:, 0:8], AX.X, ALU.max),
                    r=("rt2",), w=("rtm1",), tag=f"rm1{tt}")
                vop(cx, lambda e: e.tensor_scalar(rt2[:, 16:24], rt2[:, 0:8], rt2[:, 8:9],
                                                  None, ALU.is_ge),
                    r=("rt2", "rtm1"), w=("rtg1",), tag=f"rg1{tt}")
                vop(cx, lambda e: e.scalar_tensor_tensor(rt2[:, 24:32], rt2[:, 16:24], -1e30,
                                                         rt2[:, 0:8], ALU.mult, ALU.add),
                    r=("rtg1", "rt2"), w=("rtl2",), tag=f"rl2{tt}")
                vop(cx, lambda e: e.tensor_reduce(rt2[:, 9:10], rt2[:, 24:32], AX.X, ALU.max),
                    r=("rtl2",), w=("rtm2",), tag=f"rm2{tt}")
                vop(cx, lambda e: e.tensor_scalar(rt2[:, 32:40], rt2[:, 0:8], rt2[:, 9:10],
                                                  None, ALU.is_ge),
                    r=("rt2", "rtm2"), w=("rtsel",), tag=f"rsel{tt}")
                vop(cx, lambda e: e.tensor_scalar(rt2[:, 10:11], rt2[:, 8:9], -1.0, None, ALU.mult),
                    r=("rtm1",), w=("rtnm",), tag=f"rnm{tt}")
                sop(cx, lambda e: e.activation(rt2[:, 40:48], rt2[:, 0:8], AF.Exp,
                                               bias=rt2[:, 10:11]),
                    r=("rt2", "rtnm"), w=("rtex",), tag=f"rex{tt}")
                vop(cx, lambda e: e.tensor_mul(rt2[:, 40:48], rt2[:, 40:48], rt2[:, 32:40]),
                    r=("rtex", "rtsel"), w=("rtex",), tag=f"rexs{tt}")
                vop(cx, lambda e: e.tensor_reduce(rt2[:, 11:12], rt2[:, 40:48], AX.X, ALU.add),
                    r=("rtex",), w=("rtss",), tag=f"rss{tt}")
                vop(cx, lambda e: e.reciprocal(rt2[:, 12:13], rt2[:, 11:12]),
                    r=("rtss",), w=("rtrc",), tag=f"rrc{tt}")
                vop(cx, lambda e, tt=tt: e.tensor_scalar(
                        bass.AP(cmbv, tt, [[128, 128], [16, 8]]), rt2[:, 40:48],
                        rt2[:, 12:13], None, ALU.mult),
                    r=("rtex", "rtrc"), w=("cmbv",), tag=f"rcmb{tt}")
            top(cx, lambda e: e.transpose(psL[:, 0:128], cmbv[:, :], cx.sb["identf"][:, :]),
                r=("cmbv", "identf"), w=("psL",), tag="ctr")
            vop(cx, lambda e: e.tensor_copy(cmbT2[:, :], psL[:, 0:128]),
                r=("psL",), w=("cmbT2",), tag="ctre")
            for ei in range(8):
                pb.dma("gpsimd", bass.AP(cx.dram["scrC"], (1 + ei) * S, [[128, 16], [1, 128]]),
                       cmbT2[16 * ei:16 * ei + 16, :], r=("cmbT2",), w=("scrC",), tag=f"cst{ei}")
            if cx.cfg.get("debug"):
                cx.dbg("l2f_lgE", lgE[0:8, :], ("lgE",))
                cx.dbg("l2f_sgrow", sgrow[0:1, :], ("sgrow",))
                cx.dbg("l2f_cmbv", cmbv[:, :], ("cmbv",))
                cx.dbg("l2f_cmbT2", cmbT2[:, :], ("cmbT2",))
        pb.barrier()

        se = ExitStack()
        wgb = cx.sbuf("wgb", 128, 8 * 1408, BF16, stack=se)
        wub = cx.sbuf("wub", 128, 8 * 1408, BF16, stack=se)
        wdbs = [cx.sbuf(f"wdb{j}", 128, NI * 128, BF16, stack=se) for j in range(2)]
        hstage = cx.sbuf("hstage", 128, NI * 512, BF16, stack=se)
        t1s = cx.sbuf("t1s", 128, 512, F32, stack=se)
        sgb = cx.sbuf("sgb", 128, 512, F32, stack=se)
        ttb = cx.sbuf("ttb", 128, 512, BF16, stack=se)
        crow = cx.sbuf("crow", 1, 512, F32, stack=se)
        for ei in range(9):
            pb.dma("gpsimd", wgb[:, :],
                   bass.AP(cx.dram["l2f_wg"], ei * 1024 * 1408,
                           [[1408, 128], [128 * 1408, 8], [1, 1408]]),
                   w=("W:wgb",), tag=f"fwg{ei}")
            pb.dma("gpsimd", wub[:, :],
                   bass.AP(cx.dram["l2f_wu"], ei * 1024 * 1408,
                           [[1408, 128], [128 * 1408, 8], [1, 1408]]),
                   w=("W:wub",), tag=f"fwu{ei}")
            for tb in range(4):
                pb.dma("sync", crow[0:1, :],
                       bass.AP(cx.dram["scrC"], ei * S + 512 * tb, [[1, 1], [1, 512]]),
                       r=("scrC",), w=("crow",), tag=f"cr{ei}_{tb}")
                top(cx, lambda e: e.matmul(psB[:, 0:512], cx.sb["ones_f"][0:1, :],
                                           crow[0:1, 0:512], start=True, stop=True),
                    r=("crow", "ones_f"), w=("psB",), tag=f"cb{ei}_{tb}")
                vop(cx, lambda e: e.tensor_copy(sgb[:, :], psB[:, 0:512]),
                    r=("psB",), w=("sgb",), wd=512, tag=f"cbe{ei}_{tb}")
                for ic in range(NI):
                    for c in range(NCH):
                        top(cx, lambda e, ic=ic, c=c, tb=tb:
                            e.matmul(psG[:, 0:512], wgb[:, 1408 * c + 128 * ic:1408 * c + 128 * ic + 128],
                                     cx.sb[f"xTb{c}"][:, 512 * tb:512 * tb + 512],
                                     start=(c == 0), stop=(c == NCH - 1)),
                            r=("W:wgb", f"xTb{c}"), w=("psG",), tag=f"hg{ei}_{tb}_{ic}")
                    for c in range(NCH):
                        top(cx, lambda e, ic=ic, c=c, tb=tb:
                            e.matmul(psU[:, 0:512], wub[:, 1408 * c + 128 * ic:1408 * c + 128 * ic + 128],
                                     cx.sb[f"xTb{c}"][:, 512 * tb:512 * tb + 512],
                                     start=(c == 0), stop=(c == NCH - 1)),
                            r=("W:wub", f"xTb{c}"), w=("psU",), tag=f"hu{ei}_{tb}_{ic}")
                    sop(cx, lambda e: e.activation(t1s[:, :], psG[:, 0:512], AF.Silu),
                        r=("psG",), w=("t1s",), wd=512, tag=f"hs{ei}_{tb}_{ic}")
                    vop(cx, lambda e, ic=ic: e.tensor_mul(hstage[:, 512 * ic:512 * ic + 512],
                                                          t1s[:, :], psU[:, 0:512]),
                        r=("t1s", "psU"), w=("hstage",), wd=512, tag=f"hm{ei}_{tb}_{ic}")
                for c in range(NCH):
                    ps, pk = psDD[c % 2]
                    wd_ = wdbs[c % 2]
                    pb.dma("gpsimd", wd_[:, :],
                           bass.AP(cx.dram["l2f_wd"], ei * 1408 * 1024 + 128 * c,
                                   [[1024, 128], [128 * 1024, NI], [1, 128]]),
                           w=(f"W:wdb{c % 2}",), tag=f"fwd{ei}_{tb}_{c}")
                    for ic in range(NI):
                        top(cx, lambda e, c=c, ic=ic, ps=ps, wd_=wd_:
                            e.matmul(ps[:, 0:512], wd_[:, 128 * ic:128 * ic + 128],
                                     hstage[:, 512 * ic:512 * ic + 512],
                                     start=(ic == 0), stop=(ic == NI - 1)),
                            r=(f"W:wdb{c % 2}", "hstage"), w=(pk,), tag=f"dn{ei}_{tb}_{c}")
                    vop(cx, lambda e, ps=ps: e.tensor_mul(ttb[:, :], ps[:, 0:512], sgb[:, :]),
                        r=(pk, "sgb"), w=("ttb",), wd=512, tag=f"dt{ei}_{tb}_{c}")
                    if ei == 0:
                        vop(cx, lambda e, c=c, tb=tb: e.tensor_copy(
                                fc[c][:, 512 * tb:512 * tb + 512], ttb[:, :]),
                            r=("ttb",), w=(f"fcb{c}",), wd=512, tag=f"df{ei}_{tb}_{c}")
                    else:
                        vop(cx, lambda e, c=c, tb=tb: e.tensor_add(
                                fc[c][:, 512 * tb:512 * tb + 512],
                                fc[c][:, 512 * tb:512 * tb + 512], ttb[:, :]),
                            r=("ttb", f"fcb{c}"), w=(f"fcb{c}",), wd=512, tag=f"df{ei}_{tb}_{c}")
        if cx.cfg.get("debug"):
            for cdbg in range(NCH):
                cx.dbg(f"l2f_fc{cdbg}", fc[cdbg][:, :], (f"fcb{cdbg}",))
        se.close()
        pb.barrier()
        with ExitStack() as s2:
            fcf = cx.sbuf("fcf", 128, 512, F32, stack=s2)
            alloc_ln_bufs(cx, s2, "r7", 512)
            for tq in range(4):
                for c in range(NCH):
                    vop(cx, lambda e, c=c, tq=tq: e.tensor_copy(fcf[:, :],
                                                                fc[c][:, 512 * tq:512 * tq + 512]),
                        r=(f"fcb{c}",), w=("fcf",), wd=512, tag=f"fcp{tq}_{c}")
                    vop(cx, lambda e, c=c, tq=tq: e.tensor_add(cx.sb[f"r7{c}"][:, 0:512],
                                                               fcf[:, :],
                                                               cx.sb[f"xT{c}"][:, 512 * tq:512 * tq + 512]),
                        r=("fcf", f"xT{c}"), w=(f"r7{c}",), wd=512, tag=f"fra{tq}_{c}")
                emit_postln(cx, "r7", 512 * tq, 512, (psG, "psG"), (psU, "psU"),
                            (psD0, "psD0"), (psD1, "psD1"), tag=f"l2fp{tq}")




def emit_sort_desc_pair(cx, crt, stmp, keys, tkeys, tag):
    pb = cx.pb
    N2 = S
    stages = []
    for L in range(1, 12):
        stages.append(("flip", 2 ** L))
        j = 2 ** L // 4
        while j >= 1:
            stages.append(("half", j))
            j //= 2
    for si, (kind, p) in enumerate(stages):
        for sl in range(2):
            x, t, xk, tk = crt[sl], stmp[sl], keys[sl], tkeys[sl]
            if kind == "flip":
                B = p; nb = N2 // B
                lo = bass.AP(x, 0, [[N2, 128], [B, nb], [1, B // 2]])
                hr = bass.AP(x, B - 1, [[N2, 128], [B, nb], [-1, B // 2]])
                tc = bass.AP(t, 0, [[N2 // 2, 128], [B // 2, nb], [1, B // 2]])
                tr = bass.AP(t, B // 2 - 1, [[N2 // 2, 128], [B // 2, nb], [-1, B // 2]])
                hi = bass.AP(x, B // 2, [[N2, 128], [B, nb], [1, B // 2]])
                vop(cx, lambda e, tc=tc, lo=lo, hr=hr: e.tensor_tensor(tc, lo, hr, ALU.min),
                    r=(xk,), w=(tk,), wd=512, tag=f"{tag}f{si}a_{sl}")
                vop(cx, lambda e, lo=lo, hr=hr: e.tensor_tensor(lo, lo, hr, ALU.max),
                    r=(xk,), w=(xk,), wd=512, tag=f"{tag}f{si}b_{sl}")
                vop(cx, lambda e, hi=hi, tr=tr: e.tensor_copy(hi, tr),
                    r=(tk,), w=(xk,), wd=512, tag=f"{tag}f{si}c_{sl}")
            else:
                j = p; npair = N2 // (2 * j)
                lo = bass.AP(x, 0, [[N2, 128], [2 * j, npair], [1, j]])
                hi = bass.AP(x, j, [[N2, 128], [2 * j, npair], [1, j]])
                tc = bass.AP(t, 0, [[N2 // 2, 128], [j, npair], [1, j]])
                vop(cx, lambda e, tc=tc, lo=lo, hi=hi: e.tensor_tensor(tc, lo, hi, ALU.min),
                    r=(xk,), w=(tk,), wd=512, tag=f"{tag}h{si}a_{sl}")
                vop(cx, lambda e, lo=lo, hi=hi: e.tensor_tensor(lo, lo, hi, ALU.max),
                    r=(xk,), w=(xk,), wd=512, tag=f"{tag}h{si}b_{sl}")
                sop(cx, lambda e, hi=hi, tc=tc: e.activation(hi, tc, AF.Copy),
                    r=(tk,), w=(xk,), wd=512, tag=f"{tag}h{si}c_{sl}")


def emit_l3_attn(cx):
    pb, nc = cx.pb, cx.nc

    with ExitStack() as sc:
        x_to_bf(cx)
        psQR = cx.psum("psQR", 128, 512, F32, stack=sc)
        psQI = cx.psum("psQI", 128, 512, F32, stack=sc)
        psKR = cx.psum("psKR", 128, 512, F32, stack=sc)
        psKI = cx.psum("psKI", 128, 512, F32, stack=sc)
        psX0 = cx.psum("psX0", 128, 512, F32, stack=sc)
        psX1 = cx.psum("psX1", 128, 512, F32, stack=sc)
        psX2 = cx.psum("psX2", 128, 512, F32, stack=sc)
        psX3 = cx.psum("psX3", 128, 512, F32, stack=sc)
        psXX = [(psX0, "psX0"), (psX1, "psX1")]

        ctxT = [cx.sbuf(f"ctxT{c}", 128, S, BF16, stack=sc) for c in range(NCH)]
        vb64 = cx.sbuf("vb64", 128, 1024, BF16, stack=sc)
        rpb = cx.sbuf("rpb", 128, 9 * 512, BF16, stack=sc)
        ipb = cx.sbuf("ipb", 128, 9 * 512, BF16, stack=sc)

        with ExitStack() as s0:
            wvb3 = cx.sbuf("wvb3", 128, 8 * 1024, BF16, stack=s0)
            bvr = cx.sbuf("bvr", 1, 1024, BF16, stack=s0)
            pb.dma("gpsimd", wvb3[:, :],
                   bass.AP(cx.dram["l3_wv"], 0, [[1024, 128], [128 * 1024, 8], [1, 1024]]),
                   w=("W:wvb3",), tag="l3ldwv")
            pb.dma("gpsimd", bvr[0:1, :], cx.dram["l3_bv"][0:1, :], w=("bvr",), tag="l3ldbv")
            for h2 in range(2):
                for c in range(NCH):
                    top(cx, lambda e, h2=h2, c=c:
                        e.matmul(psX0[0:64, 0:512], cx.sb[f"xTb{c}"][:, 0:64],
                                 wvb3[:, 1024 * c + 512 * h2:1024 * c + 512 * h2 + 512],
                                 start=(c == 0), stop=False),
                        r=("W:wvb3", f"xTb{c}"), w=("psX0",), tag=f"l3v{h2}")
                top(cx, lambda e, h2=h2: e.matmul(psX0[0:64, 0:512], cx.sb["ones_bh"][0:1, 0:64],
                                                  bvr[0:1, 512 * h2:512 * h2 + 512],
                                                  start=False, stop=True),
                    r=("bvr", "ones_bh"), w=("psX0",), tag=f"l3vb{h2}")
                vop(cx, lambda e, h2=h2: e.tensor_copy(vb64[0:64, 512 * h2:512 * h2 + 512],
                                                       psX0[0:64, 0:512]),
                    r=("psX0",), w=("vb64",), wd=512, tag=f"l3ve{h2}")
                vop(cx, lambda e, h2=h2: e.tensor_copy(vb64[64:128, 512 * h2:512 * h2 + 512],
                                                       psX0[0:64, 0:512]),
                    r=("psX0",), w=("vb64",), wd=512, tag=f"l3ve2{h2}")
        pb.barrier()

        for hf in range(2):
            with ExitStack() as sa:
                qnat = cx.sbuf("qnat", 128, 16 * 512, BF16, stack=sa)
                knat = cx.sbuf("knat", 128, 16 * 512, BF16, stack=sa)
                bqr = cx.sbuf("bqr", 1, 1024, BF16, stack=sa)
                sa1 = ExitStack()
                wpq = cx.sbuf("wpq", 128, 8 * 512, BF16, stack=sa1)
                wpk = cx.sbuf("wpk", 128, 8 * 512, BF16, stack=sa1)
                pb.dma("gpsimd", bqr[0:1, 0:512],
                       bass.AP(cx.dram["l3_bq"], 512 * hf, [[1, 1], [1, 512]]),
                       w=("bqr",), tag=f"l3bq{hf}")
                pb.dma("gpsimd", bqr[0:1, 512:1024],
                       bass.AP(cx.dram["l3_bk"], 512 * hf, [[1, 1], [1, 512]]),
                       w=("bqr",), tag=f"l3bk{hf}")
                for wnm, wdst, wkey, ndst, nkey, bo_ in (("l3_wq", wpq, "W:wpq", qnat, "qnat", 0),
                                                        ("l3_wk", wpk, "W:wpk", knat, "knat", 512)):
                    pb.dma("gpsimd", wdst[:, :],
                           bass.AP(cx.dram[wnm], 512 * hf, [[1024, 128], [128 * 1024, 8], [1, 512]]),
                           w=(wkey,), tag=f"l3lw{wnm}{hf}")
                    for st_ in range(16):
                        ps, pk = psXX[st_ % 2]
                        for c in range(NCH):
                            top(cx, lambda e, st_=st_, c=c, ps=ps, wdst=wdst:
                                e.matmul(ps[:, 0:512], cx.sb[f"xTb{c}"][:, 128 * st_:128 * st_ + 128],
                                         wdst[:, 512 * c:512 * c + 512],
                                         start=(c == 0), stop=False),
                                r=(wkey, f"xTb{c}"), w=(pk,), tag=f"l3p{wnm}{hf}_{st_}")
                        top(cx, lambda e, ps=ps, bo_=bo_, bqr=bqr: e.matmul(
                                ps[:, 0:512], cx.sb["ones_bh"][0:1, 0:128],
                                bqr[0:1, bo_:bo_ + 512], start=False, stop=True),
                            r=("bqr", "ones_bh"), w=(pk,), tag=f"l3pb{wnm}{hf}_{st_}")
                        vop(cx, lambda e, st_=st_, ps=ps, ndst=ndst:
                            e.tensor_copy(ndst[:, 512 * st_:512 * st_ + 512], ps[:, 0:512]),
                            r=(pk,), w=(nkey,), wd=512, tag=f"l3pe{wnm}{hf}_{st_}")
                sa1.close()
                pb.barrier()
                dfbC = cx.sbuf("dfbC", 128, 16 * 128, BF16, stack=sa)
                dfbS = cx.sbuf("dfbS", 128, 16 * 128, BF16, stack=sa)
                qrS = cx.sbuf("qrS", 128, 512, F32, stack=sa)
                qiS = cx.sbuf("qiS", 128, 512, F32, stack=sa)
                tA = cx.sbuf("tA", 128, 512, F32, stack=sa)
                tB = cx.sbuf("tB", 128, 512, F32, stack=sa)
                for ff in range(9):
                    pb.dma("gpsimd", dfbC[:, :],
                           bass.AP(cx.dram["l3_dftc"], 128 * ff,
                                   [[2304, 128], [128 * 2304, 16], [1, 128]]),
                           w=("W:dfbC",), tag=f"l3dc{hf}_{ff}")
                    pb.dma("gpsimd", dfbS[:, :],
                           bass.AP(cx.dram["l3_dftc"], 1152 + 128 * ff,
                                   [[2304, 128], [128 * 2304, 16], [1, 128]]),
                           w=("W:dfbS",), tag=f"l3ds{hf}_{ff}")
                    for psn, pkn, db, dbk, src, srck in ((psQR, "psQR", dfbC, "W:dfbC", qnat, "qnat"),
                                                        (psQI, "psQI", dfbS, "W:dfbS", qnat, "qnat"),
                                                        (psKR, "psKR", dfbC, "W:dfbC", knat, "knat"),
                                                        (psKI, "psKI", dfbS, "W:dfbS", knat, "knat")):
                        for sc16 in range(16):
                            top(cx, lambda e, psn=psn, db=db, src=src, sc16=sc16:
                                e.matmul(psn[:, 0:512], db[:, 128 * sc16:128 * sc16 + 128],
                                         src[:, 512 * sc16:512 * sc16 + 512],
                                         start=(sc16 == 0), stop=(sc16 == 15)),
                                r=(dbk, srck), w=(pkn,), tag=f"l3f{hf}_{ff}_{pkn}")
                    vop(cx, lambda e, qrS=qrS: e.tensor_copy(qrS[:, :], psQR[:, 0:512]),
                        r=("psQR",), w=("qrS",), wd=512, tag=f"l3eq{hf}_{ff}")
                    vop(cx, lambda e, qiS=qiS: e.tensor_copy(qiS[:, :], psQI[:, 0:512]),
                        r=("psQI",), w=("qiS",), wd=512, tag=f"l3ei{hf}_{ff}")
                    vop(cx, lambda e, tA=tA, qrS=qrS: e.tensor_mul(tA[:, :], qrS[:, :], psKR[:, 0:512]),
                        r=("qrS", "psKR"), w=("tA",), wd=512, tag=f"l3r1{hf}_{ff}")
                    vop(cx, lambda e, tB=tB, qiS=qiS: e.tensor_mul(tB[:, :], qiS[:, :], psKI[:, 0:512]),
                        r=("qiS", "psKI"), w=("tB",), wd=512, tag=f"l3r2{hf}_{ff}")
                    vop(cx, lambda e, ff=ff, tA=tA, tB=tB: e.tensor_add(rpb[:, 512 * ff:512 * ff + 512],
                                                          tA[:, :], tB[:, :]),
                        r=("tA", "tB"), w=("rpb",), wd=512, tag=f"l3r3{hf}_{ff}")
                    vop(cx, lambda e, tA=tA, qiS=qiS: e.tensor_mul(tA[:, :], qiS[:, :], psKR[:, 0:512]),
                        r=("qiS", "psKR"), w=("tA",), wd=512, tag=f"l3i1{hf}_{ff}")
                    vop(cx, lambda e, tB=tB, qrS=qrS: e.tensor_mul(tB[:, :], qrS[:, :], psKI[:, 0:512]),
                        r=("qrS", "psKI"), w=("tB",), wd=512, tag=f"l3i2{hf}_{ff}")
                    vop(cx, lambda e, ff=ff, tA=tA, tB=tB: e.tensor_sub(ipb[:, 512 * ff:512 * ff + 512],
                                                          tA[:, :], tB[:, :]),
                        r=("tA", "tB"), w=("ipb",), wd=512, tag=f"l3i3{hf}_{ff}")
            pb.barrier()
            with ExitStack() as sb3:
                icb = cx.sbuf("icb", 128, 18 * 512, BF16, stack=sb3)
                crt = [cx.sbuf(f"crt{sl}", 128, S, F32, stack=sb3) for sl in range(2)]
                stmp = [cx.sbuf(f"stmp{sl}", 128, S // 2, F32, stack=sb3) for sl in range(2)]
                smb2 = [cx.sbuf(f"smb2_{sl}", 128, S, BF16, stack=sb3) for sl in range(2)]
                nmx = cx.sbuf("nmx", 128, 4, F32, stack=sb3)
                for icp in (0, 2):
                    for tq in range(4):
                        pb.dma("gpsimd", icb[:, :],
                               bass.AP(cx.dram["l3_icis"], 512 * tq,
                                       [[2048, 128], [128 * 2048, 18], [1, 512]]),
                               w=("W:icb",), tag=f"l3ic{hf}_{icp}_{tq}")
                        for sl in range(2):
                            ic = icp + sl
                            ps, pk = psXX[sl]
                            for fc in range(9):
                                top(cx, lambda e, fc=fc, ic=ic, tq=tq, ps=ps, icb=icb:
                                    e.matmul(ps[:, 0:512], rpb[:, 512 * fc + 128 * ic:512 * fc + 128 * ic + 128],
                                             icb[:, 512 * fc:512 * fc + 512],
                                             start=(fc == 0), stop=False),
                                    r=("W:icb", "rpb"), w=(pk,), tag=f"l3c{hf}_{ic}_{tq}")
                            for fc in range(9):
                                top(cx, lambda e, fc=fc, ic=ic, tq=tq, ps=ps, icb=icb:
                                    e.matmul(ps[:, 0:512], ipb[:, 512 * fc + 128 * ic:512 * fc + 128 * ic + 128],
                                             icb[:, 512 * (9 + fc):512 * (9 + fc) + 512],
                                             start=False, stop=(fc == 8)),
                                    r=("W:icb", "ipb"), w=(pk,), tag=f"l3c{hf}_{ic}_{tq}")
                            vop(cx, lambda e, sl=sl, tq=tq, ps=ps, crt=crt:
                                e.tensor_copy(crt[sl][:, 512 * tq:512 * tq + 512], ps[:, 0:512]),
                                r=(pk,), w=(f"crt{sl}",), wd=512, tag=f"l3ce{hf}_{icp}_{sl}_{tq}")
                    emit_sort_desc_pair(cx, crt, stmp, ("crt0", "crt1"), ("stmp0", "stmp1"),
                                        f"l3s{hf}_{icp}")
                    for sl in range(2):
                        cg = 4 * hf + icp + sl
                        vop(cx, lambda e, sl=sl, nmx=nmx, crt=crt: e.tensor_scalar(nmx[:, sl:sl + 1],
                                                                 crt[sl][:, 0:1], -1.0, None, ALU.mult),
                            r=(f"crt{sl}",), w=(f"nmx{sl}",), tag=f"l3nm{hf}_{icp}_{sl}")
                        sop(cx, lambda e, sl=sl, smb2=smb2, crt=crt, nmx=nmx: e.activation(smb2[sl][:, :], crt[sl][:, :], AF.Exp,
                                                              bias=nmx[:, sl:sl + 1]),
                            r=(f"crt{sl}", f"nmx{sl}"), w=(f"smb2_{sl}",), wd=512,
                            tag=f"l3ex{hf}_{icp}_{sl}")
                        vop(cx, lambda e, sl=sl, nmx=nmx, smb2=smb2: e.tensor_reduce(nmx[:, 2 + sl:3 + sl],
                                                                 smb2[sl][:, :], AX.X, ALU.add),
                            r=(f"smb2_{sl}",), w=(f"nden{sl}",), tag=f"l3dn{hf}_{icp}_{sl}")
                        vop(cx, lambda e, sl=sl, nmx=nmx: e.reciprocal(nmx[:, 2 + sl:3 + sl],
                                                              nmx[:, 2 + sl:3 + sl]),
                            r=(f"nden{sl}",), w=(f"nden{sl}",), tag=f"l3rc{hf}_{icp}_{sl}")
                        vop(cx, lambda e, sl=sl, smb2=smb2, nmx=nmx: e.tensor_scalar(smb2[sl][:, :], smb2[sl][:, :],
                                                                 nmx[:, 2 + sl:3 + sl], None, ALU.mult),
                            r=(f"smb2_{sl}", f"nden{sl}"), w=(f"smb2_{sl}",), wd=512,
                            tag=f"l3nr{hf}_{icp}_{sl}")
                        for hp2 in range(2):
                            h = 2 * cg + hp2
                            for tq in range(4):
                                top(cx, lambda e, sl=sl, hp2=hp2, h=h, tq=tq, smb2=smb2:
                                    e.matmul(psX2[0:64, 0:512], vb64[64 * hp2:64 * hp2 + 64, 64 * h:64 * h + 64],
                                             smb2[sl][64 * hp2:64 * hp2 + 64, 512 * tq:512 * tq + 512],
                                             start=True, stop=True),
                                    r=(f"smb2_{sl}", "vb64"), w=("psX2",), tag=f"l3pv{h}_{tq}")
                                vop(cx, lambda e, cg=cg, hp2=hp2, tq=tq:
                                    e.tensor_copy(ctxT[cg][64 * hp2:64 * hp2 + 64, 512 * tq:512 * tq + 512],
                                                  psX2[0:64, 0:512]),
                                    r=("psX2",), w=(f"ctxT{cg}",), wd=512, tag=f"l3pe{h}_{tq}")
            pb.barrier()
        with ExitStack() as s2:
            wob = cx.sbuf("wob", 128, 8 * 512, BF16, stack=s2)
            borow = cx.sbuf("borow", 1, 1024, BF16, stack=s2)
            alloc_ln_bufs(cx, s2, "r8", 512)
            pb.dma("gpsimd", borow[0:1, :], cx.dram["l3_bo"][0:1, :], w=("borow",), tag="l3ldbo")
            for tq in range(4):
                for h_ in range(2):
                    pb.dma("gpsimd", wob[:, :],
                           bass.AP(cx.dram["l3_wo"], 512 * h_,
                                   [[1024, 128], [128 * 1024, 8], [1, 512]]),
                           w=("W:wob",), tag=f"l3lo{tq}_{h_}")
                    for ei in range(4):
                        ec = 4 * h_ + ei
                        ps, pk = psXX[ec % 2]
                        for hc in range(8):
                            top(cx, lambda e, ei=ei, hc=hc, ps=ps, tq=tq:
                                e.matmul(ps[:, 0:512], wob[:, 512 * hc + 128 * ei:512 * hc + 128 * ei + 128],
                                         ctxT[hc][:, 512 * tq:512 * tq + 512],
                                         start=(hc == 0), stop=False),
                                r=("W:wob", f"ctxT{hc}"), w=(pk,), tag=f"l3o{tq}_{ec}")
                        top(cx, lambda e, ec=ec, ps=ps: e.matmul(
                                ps[:, 0:512], borow[0:1, 128 * ec:128 * ec + 128],
                                cx.sb["ones_bh"][0:1, 0:512], start=False, stop=True),
                            r=("borow", "ones_bh"), w=(pk,), tag=f"l3ob{tq}_{ec}")
                        vop(cx, lambda e, ec=ec, ps=ps, tq=tq:
                            e.tensor_add(cx.sb[f"r8{ec}"][:, 0:512], ps[:, 0:512],
                                         cx.sb[f"xT{ec}"][:, 512 * tq:512 * tq + 512]),
                            r=(pk, f"xT{ec}"), w=(f"r8{ec}",), wd=512, tag=f"l3fr{tq}_{ec}")
                emit_postln(cx, "r8", 512 * tq, 512, (psQR, "psQR"), (psQI, "psQI"),
                            (psKR, "psKR"), (psKI, "psKI"), tag=f"l3p{tq}")




def emit_l3_ffn(cx):
    pb, nc = cx.pb, cx.nc

    with ExitStack() as sc:
        x_to_bf(cx)
        psG3 = cx.psum("psG3", 128, 512, F32, stack=sc)
        psU3 = cx.psum("psU3", 128, 512, F32, stack=sc)
        psH0 = cx.psum("psH0", 128, 512, F32, stack=sc)
        psH1 = cx.psum("psH1", 128, 512, F32, stack=sc)
        psHH = [(psH0, "psH0"), (psH1, "psH1")]
        scl = cx.sbuf("scl", 128, 136, F32, stack=sc)
        pb.dma("sync", scl[:, 0:64], cx.dram["l3f_s12"][:, :], w=("scl",), tag="lds12")
        pb.dma("sync", scl[:, 64:128], cx.dram["l3f_sh"][:, :], w=("scl",), tag="ldsh")
        pb.dma("sync", scl[:, 128:136], cx.dram["l3f_s3"][:, :], w=("scl",), tag="lds3")
        hA = cx.sbuf("hA", 128, 32 * 512, BF16, stack=sc)
        hB = cx.sbuf("hB", 128, 32 * 512, BF16, stack=sc)
        whs = [cx.sbuf(f"whs{j}", 128, 32 * 128, BF16, stack=sc) for j in range(2)]
        w12s = whs
        t2s = cx.sbuf("t2s", 128, 512, F32, stack=sc)
        t1s3 = cx.sbuf("t1s3", 128, 512, F32, stack=sc)
        tys = t2s
        alloc_ln_bufs(cx, sc, "r9", 512)

        for tb in range(4):
            sb0 = 512 * tb
            for oc in range(32):
                ws, wk = w12s[oc % 2], f"W:whs{oc % 2}"
                pb.dma("gpsimd", ws[:, 0:1024],
                       bass.AP(cx.dram["l3f_w1t"], 128 * oc,
                               [[4096, 128], [128 * 4096, 8], [1, 128]]),
                       w=(wk,), tag=f"t1w{tb}_{oc}")
                pb.dma("gpsimd", ws[:, 1024:2048],
                       bass.AP(cx.dram["l3f_w2t"], 128 * oc,
                               [[4096, 128], [128 * 4096, 8], [1, 128]]),
                       w=(wk,), tag=f"t2w{tb}_{oc}")
                for c in range(NCH):
                    top(cx, lambda e, c=c, ws=ws, sb0=sb0:
                        e.matmul(psG3[:, 0:512], ws[:, 128 * c:128 * c + 128],
                                 cx.sb[f"xTb{c}"][:, sb0:sb0 + 512],
                                 start=(c == 0), stop=(c == NCH - 1)),
                        r=(wk, f"xTb{c}"), w=("psG3",), tag=f"m1{tb}_{oc}")
                for c in range(NCH):
                    top(cx, lambda e, c=c, ws=ws, sb0=sb0:
                        e.matmul(psU3[:, 0:512], ws[:, 1024 + 128 * c:1024 + 128 * c + 128],
                                 cx.sb[f"xTb{c}"][:, sb0:sb0 + 512],
                                 start=(c == 0), stop=(c == NCH - 1)),
                        r=(wk, f"xTb{c}"), w=("psU3",), tag=f"m2{tb}_{oc}")
                sop(cx, lambda e, oc=oc: e.activation(t1s3[:, :], psG3[:, 0:512], AF.Silu,
                                                      scale=scl[:, oc:oc + 1]),
                    r=("psG3", "scl"), w=("t1s3",), wd=512, tag=f"e1{tb}_{oc}")
                vop(cx, lambda e, oc=oc: e.tensor_scalar(t2s[:, :], psU3[:, 0:512],
                                                         scl[:, 32 + oc:33 + oc], None, ALU.mult),
                    r=("psU3", "scl"), w=("t2s",), wd=512, tag=f"e2{tb}_{oc}")
                vop(cx, lambda e, oc=oc: e.tensor_mul(hA[:, 512 * oc:512 * oc + 512],
                                                      t1s3[:, :], t2s[:, :]),
                    r=("t1s3", "t2s"), w=("hA",), wd=512, tag=f"e3{tb}_{oc}")
            for o2 in range(32):
                ws, wk = whs[o2 % 2], f"W:whs{o2 % 2}"
                pb.dma("gpsimd", ws[:, :],
                       bass.AP(cx.dram["l3f_h1t"], 128 * o2,
                               [[4096, 128], [128 * 4096, 32], [1, 128]]),
                       w=(wk,), tag=f"h1w{tb}_{o2}")
                ps, pk = psHH[o2 % 2]
                for oc in range(32):
                    top(cx, lambda e, oc=oc, ws=ws, ps=ps:
                        e.matmul(ps[:, 0:512], ws[:, 128 * oc:128 * oc + 128],
                                 hA[:, 512 * oc:512 * oc + 512],
                                 start=(oc == 0), stop=(oc == 31)),
                        r=(wk, "hA"), w=(pk,), tag=f"mh1{tb}_{o2}")
                sop(cx, lambda e, o2=o2, ps=ps: e.activation(hB[:, 512 * o2:512 * o2 + 512],
                                                             ps[:, 0:512], AF.Gelu,
                                                             scale=scl[:, 64 + o2:65 + o2]),
                    r=(pk, "scl"), w=("hB",), wd=512, tag=f"eh1{tb}_{o2}")
            for o2 in range(32):
                ws, wk = whs[o2 % 2], f"W:whs{o2 % 2}"
                pb.dma("gpsimd", ws[:, :],
                       bass.AP(cx.dram["l3f_h2t"], 128 * o2,
                               [[4096, 128], [128 * 4096, 32], [1, 128]]),
                       w=(wk,), tag=f"h2w{tb}_{o2}")
                ps, pk = psHH[o2 % 2]
                for oc in range(32):
                    top(cx, lambda e, oc=oc, ws=ws, ps=ps:
                        e.matmul(ps[:, 0:512], ws[:, 128 * oc:128 * oc + 128],
                                 hB[:, 512 * oc:512 * oc + 512],
                                 start=(oc == 0), stop=(oc == 31)),
                        r=(wk, "hB"), w=(pk,), tag=f"mh2{tb}_{o2}")
                sop(cx, lambda e, o2=o2, ps=ps: e.activation(hA[:, 512 * o2:512 * o2 + 512],
                                                             ps[:, 0:512], AF.Gelu,
                                                             scale=scl[:, 96 + o2:97 + o2]),
                    r=(pk, "scl"), w=("hA",), wd=512, tag=f"eh2{tb}_{o2}")
            for ec in range(NCH):
                ws, wk = whs[ec % 2], f"W:whs{ec % 2}"
                pb.dma("gpsimd", ws[:, :],
                       bass.AP(cx.dram["l3f_w3t"], 128 * ec,
                               [[1024, 128], [128 * 1024, 32], [1, 128]]),
                       w=(wk,), tag=f"w3w{tb}_{ec}")
                ps, pk = psHH[ec % 2]
                for oc in range(32):
                    top(cx, lambda e, oc=oc, ws=ws, ps=ps:
                        e.matmul(ps[:, 0:512], ws[:, 128 * oc:128 * oc + 128],
                                 hA[:, 512 * oc:512 * oc + 512],
                                 start=(oc == 0), stop=(oc == 31)),
                        r=(wk, "hA"), w=(pk,), tag=f"mw3{tb}_{ec}")
                vop(cx, lambda e, ec=ec, ps=ps: e.tensor_scalar(tys[:, :], ps[:, 0:512],
                                                                scl[:, 128 + ec:129 + ec],
                                                                None, ALU.mult),
                    r=(pk, "scl"), w=("t2s",), wd=512, tag=f"ey{tb}_{ec}")
                vop(cx, lambda e, ec=ec, sb0=sb0: e.tensor_add(cx.sb[f"r9{ec}"][:, 0:512],
                                                               tys[:, :],
                                                               cx.sb[f"xT{ec}"][:, sb0:sb0 + 512]),
                    r=("t2s", f"xT{ec}"), w=(f"r9{ec}",), wd=512, tag=f"fr3{tb}_{ec}")
            emit_postln(cx, "r9", sb0, 512, (psG3, "psG3"), (psU3, "psU3"),
                        (psH0, "psH0"), (psH1, "psH1"), tag=f"l3fp{tb}")


def emit_l1_attn(cx):
    pb, nc = cx.pb, cx.nc
    HH, HD, LD, R = 8, 64, 512, 4
    TC = S // R
    SCL = 0.125

    with ExitStack() as sc:
        wl = cx.sbuf("wl", 128, 4 * 1536, BF16, stack=sc)
        wlp = cx.sbuf("wlp", 128, 4 * 512, BF16, stack=sc)
        wgt = cx.sbuf("wgt", 128, 8 * 2, BF16, stack=sc)
        bias = cx.sbuf("bias", 128, 40, F32, stack=sc)
        lvbr = cx.sbuf("lvbr", 1, 2 * 520, F32, stack=sc)
        lvb = cx.sbuf("lvb", 128, 520, F32, stack=sc)
        gvb = cx.sbuf("gvb", 128, 520, F32, stack=sc)
        bgate = cx.sbuf("bgate", 1, 4, F32, stack=sc)
        gtt = [cx.sbuf(f"gtt{t}", 128, 512, BF16, stack=sc) for t in range(4)]
        rep4 = cx.sbuf("rep4", 128, 512, BF16, stack=sc)
        trimask = cx.sbuf("trimask", 128, 128, F32, stack=sc)
        for nm, dn, nch, wdt in (("wl", "l1_wlqkv", 4, 1536),
                                 ("wlp", "l1_wlp", 4, 512), ("wgt", "l1_wgate", 8, 2)):
            pb.dma("gpsimd", cx.sb[nm][:, :],
                   bass.AP(cx.dram[dn], 0, [[wdt, 128], [128 * wdt, nch], [1, wdt]]),
                   w=(f"W:{nm}",), tag=f"ld{nm}")
        pb.dma("gpsimd", bias[:, :], cx.dram["l1_biases"][:, :], w=("bias",), tag="ldb")
        pb.dma("sync", lvbr[:, :], cx.dram["l1_vbias"][:, :], w=("lvbr",), tag="ldvb")
        pb.dma("sync", bgate[0:1, 0:2], cx.dram["l1_bgate"][:, :], w=("bgate",), tag="ldbg")
        pb.dma("sync", trimask[:, :], cx.dram["c_tri"][:, :], w=("trimask",), tag="ldtri")
        pb.dma("gpsimd", rep4[:, :], cx.dram["c_rep4"][:, :], w=("rep4",), tag="ldrep4")

        x_to_bf(cx)

        lk = [cx.sbuf(f"lk{c}", 128, S, BF16, stack=sc) for c in range(4)]
        lv = [cx.sbuf(f"lv{n}", 128, 8 * 65, BF16, stack=sc) for n in range(16)]
        pt = cx.sbuf("pt", 128, 2 * 512, BF16, stack=sc)
        yden = cx.sbuf("yden", 1, 512, F32, stack=sc)
        ytmp = cx.sbuf("ytmp", 64, 512, F32, stack=sc)
        bgd = cx.sbuf("bgd", 1, 1, F32, stack=sc)

        psA0 = cx.psum("psA0", 128, 1024, F32, stack=sc)
        psA1 = cx.psum("psA1", 128, 512, F32, stack=sc)
        psB = cx.psum("psB", 128, 1024, F32, stack=sc)
        psC = cx.psum("psC", 128, 512, F32, stack=sc)
        psD = cx.psum("psD", 128, 512, F32, stack=sc)
        psE = cx.psum("psE", 128, 512, F32, stack=sc)
        psCD = [(psC, "psC"), (psD, "psD")]

        for n in range(16):
            pb.op("vector", lambda e, n=n: e.memset(lv[n][:, :], 1.0), w=(f"lv{n}",))
        for (dst, key, off) in ((lvb, "lvb", 0), (gvb, "gvb", 520)):
            top(cx, lambda e, off=off: e.matmul(psE[:, 0:512], cx.sb["ones_f"][0:1, :],
                                                lvbr[0:1, off:off + 512], start=True, stop=True),
                r=("lvbr", "ones_f"), w=("psE",), tag=f"vb{key}a")
            vop(cx, lambda e, dst=dst: e.tensor_copy(dst[:, 0:512], psE[:, 0:512]),
                r=("psE",), w=(key,), wd=512, tag=f"vb{key}ac")
            top(cx, lambda e, off=off: e.matmul(psE[:, 0:8], cx.sb["ones_f"][0:1, :],
                                                lvbr[0:1, off + 512:off + 520], start=True, stop=True),
                r=("lvbr", "ones_f"), w=("psE",), tag=f"vb{key}b")
            vop(cx, lambda e, dst=dst: e.tensor_copy(dst[:, 512:520], psE[:, 0:8]),
                r=("psE",), w=(key,), tag=f"vb{key}bc")
        vop(cx, lambda e: e.tensor_sub(bgd[0:1, 0:1], bgate[0:1, 0:1], bgate[0:1, 1:2]),
            r=("bgate",), w=("bgd",), tag="bgd")

        for oc in range(4):
            for tb in range(4):
                ps, pk = psCD[tb % 2]
                for c in range(4):
                    top(cx, lambda e, oc=oc, tb=tb, c=c, ps=ps:
                        e.matmul(ps[:, 0:512], wl[:, 1536 * c + 512 + 128 * oc:1536 * c + 512 + 128 * oc + 128],
                                 cx.sb[f"xTb{c}"][:, 512 * tb:512 * tb + 512],
                                 start=(c == 0), stop=(c == 3)),
                        r=("W:wl", f"xTb{c}"), w=(pk,), tag=f"lk{oc}_{tb}")
                vop(cx, lambda e, oc=oc, tb=tb, ps=ps:
                    e.tensor_scalar(lk[oc][:, 512 * tb:512 * tb + 512], ps[:, 0:512],
                                    bias[:, 4 + oc:5 + oc], None, ALU.add),
                    r=(pk, "bias"), w=(f"lk{oc}",), wd=512, tag=f"lke{oc}_{tb}")
        for tc_i in range(16):
            ps, pk = psCD[tc_i % 2]
            for hh in range(8):
                for c in range(4):
                    top(cx, lambda e, tc_i=tc_i, hh=hh, c=c, ps=ps:
                        e.matmul(ps[:, 64 * hh:64 * hh + 64],
                                 cx.sb[f"xTb{c}"][:, 128 * tc_i:128 * tc_i + 128],
                                 wl[:, 1536 * c + 1024 + 64 * hh:1536 * c + 1024 + 64 * hh + 64],
                                 start=(c == 0), stop=(c == 3)),
                        r=("W:wl", f"xTb{c}"), w=(pk,), tag=f"lv{tc_i}_{hh}")
            vop(cx, lambda e, tc_i=tc_i, ps=ps: e.tensor_tensor(
                    bass.AP(lv[tc_i], 0, [[520, 128], [65, 8], [1, 64]]),
                    bass.AP(ps, 0, [[512, 128], [64, 8], [1, 64]]),
                    bass.AP(lvb, 0, [[520, 128], [65, 8], [1, 64]]), ALU.add),
                r=(pk, "lvb"), w=(f"lv{tc_i}",), wd=512, tag=f"lve{tc_i}")
        sg_scope = ExitStack()
        wg = cx.sbuf("wg", 128, 4 * 1536, BF16, stack=sg_scope)
        wc = cx.sbuf("wc", 128, 2048, BF16, stack=sg_scope)
        cg = [cx.sbuf(f"cg{c}", 128, TC, BF16, stack=sg_scope) for c in range(4)]
        gq = cx.sbuf("gq", 128, 4 * TC, BF16, stack=sg_scope)
        gk = cx.sbuf("gk", 128, 4 * TC, BF16, stack=sg_scope)
        gv = [cx.sbuf(f"gv{n}", 128, 8 * 65, BF16, stack=sg_scope) for n in range(4)]
        gy = cx.sbuf("gy", 128, 4 * TC, BF16, stack=sg_scope)
        wgp = cx.sbuf("wgp", 128, 4 * 512, BF16, stack=sg_scope)
        pb.dma("gpsimd", wgp[:, :],
               bass.AP(cx.dram["l1_wgp"], 0, [[512, 128], [128 * 512, 4], [1, 512]]),
               w=("W:wgp",), tag="ldwgp")
        pb.dma("gpsimd", wg[:, :],
               bass.AP(cx.dram["l1_wgqkv"], 0, [[1536, 128], [128 * 1536, 4], [1, 1536]]),
               w=("W:wg",), tag="ldwg")
        for n in range(4):
            pb.op("vector", lambda e, n=n: e.memset(gv[n][:, :], 1.0), w=(f"gv{n}",))
        for oc in range(4):
            ps, pk = psCD[oc % 2]
            for hf in range(2):
                pb.dma("gpsimd", wc[:, :],
                       bass.AP(cx.dram["l1_wcomp"], 16 * 128 * 1024 * hf + 128 * (4 + oc),
                               [[1024, 128], [128 * 1024, 16], [1, 128]]),
                       w=("W:wc",), tag=f"ldwc{oc}_{hf}")
                for ks in range(16):
                    kc = 16 * hf + ks
                    r_, dc = kc // 8, kc % 8
                    top(cx, lambda e, ks=ks, kc=kc, r_=r_, dc=dc, ps=ps:
                        e.matmul(ps[:, 0:TC], wc[:, 128 * ks:128 * ks + 128],
                                 bass.AP(cx.sb[f"xTb{dc}"], r_, [[S, 128], [4, TC]]),
                                 start=(kc == 0), stop=(kc == 31)),
                        r=("W:wc", f"xTb{dc}"), w=(pk,), tag=f"cp{oc}_{kc}")
            vop(cx, lambda e, oc=oc, ps=ps: e.tensor_scalar(cg[oc][:, :], ps[:, 0:TC],
                                                            bias[:, 28 + oc:29 + oc], None, ALU.add),
                r=(pk, "bias"), w=(f"cg{oc}",), wd=512, tag=f"cpe{oc}")
        for oc in range(4):
            for (dst, key, off, bo) in ((gq, "gq", 0, 12), (gk, "gk", 512, 16)):
                ps, pk = psCD[oc % 2]
                for c in range(4):
                    top(cx, lambda e, oc=oc, c=c, ps=ps, off=off:
                        e.matmul(ps[:, 0:TC], wg[:, 1536 * c + off + 128 * oc:1536 * c + off + 128 * oc + 128],
                                 cg[c][:, :], start=(c == 0), stop=(c == 3)),
                        r=("W:wg", f"cg{c}"), w=(pk,), tag=f"gp{key}{oc}")
                vop(cx, lambda e, oc=oc, ps=ps, dst=dst, key=key, bo=bo:
                    e.tensor_scalar(dst[:, TC * oc:TC * oc + TC], ps[:, 0:TC],
                                    bias[:, bo + oc:bo + oc + 1], None, ALU.add),
                    r=(pk, "bias"), w=(key,), wd=512, tag=f"ge{key}{oc}")
        for tc_i in range(4):
            ps, pk = psCD[tc_i % 2]
            for hh in range(8):
                for c in range(4):
                    top(cx, lambda e, tc_i=tc_i, hh=hh, c=c, ps=ps:
                        e.matmul(ps[:, 64 * hh:64 * hh + 64],
                                 cg[c][:, 128 * tc_i:128 * tc_i + 128],
                                 wg[:, 1536 * c + 1024 + 64 * hh:1536 * c + 1024 + 64 * hh + 64],
                                 start=(c == 0), stop=(c == 3)),
                        r=("W:wg", f"cg{c}"), w=(pk,), tag=f"gvv{tc_i}_{hh}")
            vop(cx, lambda e, tc_i=tc_i, ps=ps: e.tensor_tensor(
                    bass.AP(gv[tc_i], 0, [[520, 128], [65, 8], [1, 64]]),
                    bass.AP(ps, 0, [[512, 128], [64, 8], [1, 64]]),
                    bass.AP(gvb, 0, [[520, 128], [65, 8], [1, 64]]), ALU.add),
                r=(pk, "gvb"), w=(f"gv{tc_i}",), wd=512, tag=f"gve{tc_i}")

        AB = {"psA0": psA0, "psA1": psA1, "psB": psB, "psE": psE,
              "pt": pt, "yden": yden, "ytmp": ytmp, "trimask": trimask}
        for hh in range(8):
            attn_head_block(cx, AB, gq, "gq",
                       (lambda n, hh=hh: gk[64 * (hh % 2):64 * (hh % 2) + 64,
                                            TC * (hh // 2) + 128 * n:TC * (hh // 2) + 128 * n + 128]),
                       "gk", gv, "gv", [0, 1, 2, 3], 0, gy, "gy", hh, SCL, f"ga{hh}")
        gob = cx.sbuf("gob", 128, TC, BF16, stack=sg_scope)
        psEb = psE.ap().bitcast(BF16)
        for oc in range(4):
            ps, pk = psCD[oc % 2]
            for c in range(4):
                top(cx, lambda e, oc=oc, c=c, ps=ps:
                    e.matmul(ps[:, 0:TC], wgp[:, 512 * c + 128 * oc:512 * c + 128 * oc + 128],
                             gy[:, TC * c:TC * c + TC], start=(c == 0), stop=(c == 3)),
                    r=("W:wgp", "gy"), w=(pk,), tag=f"gpj{oc}")
            vop(cx, lambda e, oc=oc, ps=ps: e.tensor_scalar(gob[:, :], ps[:, 0:TC],
                                                            bias[:, 36 + oc:37 + oc], None, ALU.add),
                r=(pk, "bias"), w=("gob",), wd=512, tag=f"gpe{oc}")
            for t in range(4):
                top(cx, lambda e, t=t: e.transpose(psEb[:, 0:128], gob[:, 128 * t:128 * t + 128],
                                                   cx.sb["ident"][:, :]),
                    r=("gob", "ident"), w=("psE",), tag=f"gptr{oc}_{t}")
                vop(cx, lambda e, t=t, oc=oc: e.tensor_copy(gtt[t][:, 128 * oc:128 * oc + 128],
                                                            psEb[:, 0:128]),
                    r=("psE",), w=(f"gtt{t}",), wd=128, tag=f"gpte{oc}_{t}")

        sg_scope.close()
        pb.barrier()
        prow = cx.sbuf("prow", 1, 3 * 512, F32, stack=sc)
        ltmp = cx.sbuf("ltmp", 128, 512, F32, stack=sc)
        lq = cx.sbuf("lq", 128, 4 * 512, BF16, stack=sc)
        ylb = cx.sbuf("ylb", 128, 4 * 512, BF16, stack=sc)
        lob = cx.sbuf("lob", 128, 4 * 512, BF16, stack=sc)
        grep = [cx.sbuf(f"grep{c}", 128, 512, BF16, stack=sc) for c in range(4)]
        alloc_ln_bufs(cx, sc, "r2", 256)
        for blk in range(4):
            sb0 = 512 * blk
            for oc in range(4):
                ps, pk = psCD[oc % 2]
                for c in range(4):
                    top(cx, lambda e, oc=oc, c=c, ps=ps, sb0=sb0:
                        e.matmul(ps[:, 0:512], wl[:, 1536 * c + 128 * oc:1536 * c + 128 * oc + 128],
                                 cx.sb[f"xTb{c}"][:, sb0:sb0 + 512],
                                 start=(c == 0), stop=(c == 3)),
                        r=("W:wl", f"xTb{c}"), w=(pk,), tag=f"lq{blk}_{oc}")
                vop(cx, lambda e, oc=oc, ps=ps: e.tensor_scalar(lq[:, 512 * oc:512 * oc + 512],
                                                                ps[:, 0:512], bias[:, oc:oc + 1],
                                                                None, ALU.add),
                    r=(pk, "bias"), w=("lq",), wd=512, tag=f"lqe{blk}_{oc}")
            for hh in range(8):
                attn_head_block(cx, AB, lq, "lq",
                           (lambda n, hh=hh: lk[hh // 2][64 * (hh % 2):64 * (hh % 2) + 64,
                                                         128 * n:128 * n + 128]),
                                f"lk{hh // 2}", lv, "lv", list(range(4 * blk + 4)), 4 * blk,
                                ylb, "ylb", hh, SCL, f"la{blk}_{hh}")
            prl = [(psA0[:, 0:512], "psA0"), (psA0[:, 512:1024], "psA0"),
                   (psB[:, 0:512], "psB"), (psB[:, 512:1024], "psB")]
            for oc in range(4):
                pps, ppk = prl[oc]
                for c in range(4):
                    top(cx, lambda e, oc=oc, c=c, pps=pps:
                        e.matmul(pps, wlp[:, 512 * c + 128 * oc:512 * c + 128 * oc + 128],
                                 ylb[:, 512 * c:512 * c + 512], start=(c == 0), stop=(c == 3)),
                        r=("W:wlp", "ylb"), w=(ppk,), tag=f"lpj{blk}_{oc}")
                sop(cx, lambda e, oc=oc, pps=pps: e.activation(lob[:, 512 * oc:512 * oc + 512],
                                                               pps, AF.Identity,
                                                               bias=bias[:, 32 + oc:33 + oc]),
                    r=(ppk, "bias"), w=("lob",), wd=512, tag=f"lpe{blk}_{oc}")
            for c in range(4):
                top(cx, lambda e, c=c, blk=blk:
                    e.matmul(psE[:, 0:512], gtt[blk][:, 128 * c:128 * c + 128],
                             rep4[:, :], start=True, stop=True),
                    r=(f"gtt{blk}", "rep4"), w=("psE",), tag=f"gr{blk}_{c}")
                vop(cx, lambda e, c=c: e.tensor_copy(grep[c][:, :], psE[:, 0:512]),
                    r=("psE",), w=(f"grep{c}",), wd=512, tag=f"gre{blk}_{c}")
            for gsel in range(2):
                ps, pk = psCD[gsel]
                for c in range(8):
                    rhs_fn = (lambda e, c=c: lob[:, 512 * c:512 * c + 512]) if c < 4 else                              (lambda e, c=c: grep[c - 4][:, :])
                    rk = "lob" if c < 4 else f"grep{c - 4}"
                    top(cx, lambda e, c=c, gsel=gsel, ps=ps, rhs_fn=rhs_fn:
                        e.matmul(ps[0:1, 0:512], wgt[:, 2 * c + gsel:2 * c + gsel + 1],
                                 rhs_fn(e), start=(c == 0), stop=(c == 7)),
                        r=("W:wgt", rk), w=(pk,), tag=f"gl{blk}_{gsel}_{c}")
            vop(cx, lambda e: e.tensor_copy(prow[0:1, 512:1024], psD[0:1, 0:512]),
                r=("psD",), w=("prow",), tag=f"gdc{blk}")
            vop(cx, lambda e: e.tensor_sub(prow[0:1, 0:512], psC[0:1, 0:512], prow[0:1, 512:1024]),
                r=("psC", "prow"), w=("prow",), tag=f"gd{blk}")
            vop(cx, lambda e: e.tensor_scalar(prow[0:1, 0:512], prow[0:1, 0:512],
                                              bgd[0:1, 0:1], None, ALU.add),
                r=("prow", "bgd"), w=("prow",), tag=f"gdb{blk}")
            sop(cx, lambda e: e.activation(prow[0:1, 512:1024], prow[0:1, 0:512], AF.Sigmoid),
                r=("prow",), w=("prow2",), tag=f"gs0{blk}")
            sop(cx, lambda e: e.activation(prow[0:1, 1024:1536], prow[0:1, 0:512], AF.Sigmoid,
                                           scale=-1.0),
                r=("prow",), w=("prow3",), tag=f"gs1{blk}")
            top(cx, lambda e: e.matmul(psC[:, 0:512], cx.sb["ones_f"][0:1, :],
                                       prow[0:1, 512:1024], start=True, stop=True),
                r=("prow2", "ones_f"), w=("psC",), tag=f"gr0{blk}")
            top(cx, lambda e: e.matmul(psD[:, 0:512], cx.sb["ones_f"][0:1, :],
                                       prow[0:1, 1024:1536], start=True, stop=True),
                r=("prow3", "ones_f"), w=("psD",), tag=f"gr1{blk}")
            for sub in range(2):
                s0 = 256 * sub
                for c in range(NCH):
                    if c < 4:
                        vop(cx, lambda e, c=c, s0=s0: e.tensor_mul(ltmp[:, 0:256],
                                                                   lob[:, 512 * c + s0:512 * c + s0 + 256],
                                                                   psC[:, s0:s0 + 256]),
                            r=("lob", "psC"), w=("ltmp",), wd=256, tag=f"go{blk}_{sub}_{c}")
                    else:
                        vop(cx, lambda e, c=c, s0=s0: e.tensor_mul(ltmp[:, 0:256],
                                                                   grep[c - 4][:, s0:s0 + 256],
                                                                   psD[:, s0:s0 + 256]),
                            r=(f"grep{c - 4}", "psD"), w=("ltmp",), wd=256, tag=f"go{blk}_{sub}_{c}")
                    vop(cx, lambda e, c=c, sb0=sb0, s0=s0: e.tensor_add(
                            cx.sb[f"r2{c}"][:, :],
                            cx.sb[f"xT{c}"][:, sb0 + s0:sb0 + s0 + 256], ltmp[:, 0:256]),
                        r=(f"xT{c}", "ltmp"), w=(f"r2{c}",), wd=256, tag=f"gra{blk}_{sub}_{c}")
                emit_postln(cx, "r2", sb0 + s0, 256, (psA1, "psA1"), (psE, "psE"),
                            (psA0, "psA0"), (psB, "psB"), tag=f"a1{blk}_{sub}")





def emit_l1_ffn(cx):
    pb, nc = cx.pb, cx.nc
    SCL = 0.125

    with ExitStack() as sc:
        pp = cx.sbuf("pp", 128, 66, F32, stack=sc)
        pb.dma("gpsimd", pp[:, :], cx.dram["l1f_pp"][:, :], w=("pp",), tag="ldpp")
        vbr = cx.sbuf("vbr", 1, 520, F32, stack=sc)
        pt = cx.sbuf("pt", 128, 2 * 512, BF16, stack=sc)
        yden = cx.sbuf("yden", 1, 512, F32, stack=sc)
        ytmp = cx.sbuf("ytmp", 64, 512, F32, stack=sc)
        rwc = cx.sbuf("rwc", 128, 8, F32, stack=sc)
        rwg = cx.sbuf("rwg", 128, 8, F32, stack=sc)
        kbc = cx.sbuf("kbc", 1, 8, F32, stack=sc)
        pb.dma("sync", rwc[:, :], cx.dram["l1f_rwc"][:, :], w=("rwc",), tag="ldrwc")

        psA0 = cx.psum("psA0", 128, 1024, F32, stack=sc)
        psA1 = cx.psum("psA1", 128, 512, F32, stack=sc)
        psB = cx.psum("psB", 128, 1024, F32, stack=sc)
        psC = cx.psum("psC", 128, 512, F32, stack=sc)
        psD = cx.psum("psD", 128, 512, F32, stack=sc)
        psE = cx.psum("psE", 128, 512, F32, stack=sc)
        psCD = [(psC, "psC"), (psD, "psD")]
        AB = {"psA0": psA0, "psA1": psA1, "psB": psB, "psE": psE,
              "pt": pt, "yden": yden, "ytmp": ytmp, "trimask": cx.sb["ones_f"]}

        g1 = lambda c: pp[:, 32 + c:33 + c]
        b1_ = lambda c: pp[:, 40 + c:41 + c]
        g2 = lambda c: pp[:, 48 + c:49 + c]
        b2_ = lambda c: pp[:, 56 + c:57 + c]

        x_to_bf(cx)
        vop(cx, lambda e: e.tensor_mul(rwg[:, :], rwc[:, :], pp[:, 32:40]),
            r=("rwc", "pp"), w=("rwg",), tag="krwg")
        top(cx, lambda e: e.matmul(psE[0:1, 0:8], cx.sb["ones_f"][:, 0:1], rwg[:, :],
                                   start=True, stop=True),
            r=("rwg", "ones_f"), w=("psE",), tag="kB1")
        vop(cx, lambda e: e.tensor_reduce(kbc[0:1, 0:1], psE[0:1, 0:8], AX.X, ALU.add),
            r=("psE",), w=("kbc",), tag="kB2")
        vop(cx, lambda e: e.tensor_mul(rwc[:, :], rwc[:, :], pp[:, 40:48]),
            r=("rwc", "pp"), w=("rwc",), tag="kC1")
        top(cx, lambda e: e.matmul(psE[0:1, 0:8], cx.sb["ones_f"][:, 0:1], rwc[:, :],
                                   start=True, stop=True),
            r=("rwc", "ones_f"), w=("psE",), tag="kC2")
        vop(cx, lambda e: e.tensor_reduce(kbc[0:1, 1:2], psE[0:1, 0:8], AX.X, ALU.add),
            r=("psE",), w=("kbc",), tag="kC3")
        vop(cx, lambda e: e.tensor_scalar(kbc[0:1, 1:2], kbc[0:1, 1:2], pp[0:1, 64:65],
                                          None, ALU.add),
            r=("kbc", "pp"), w=("kbc",), tag="kC4")

        with ExitStack() as sa:
            wfull = cx.sbuf("wfull", 128, 4096, BF16, stack=sa)
            kT = [cx.sbuf(f"fkT{c}", 128, S, BF16, stack=sa) for c in range(NCH)]
            va = [cx.sbuf(f"fva{n}", 128, 16 * 65, BF16, stack=sa) for n in range(16)]
            qT = cx.sbuf("fqT", 128, 2048, BF16, stack=sa)
            krowa = cx.sbuf("krowA", 1, 256, F32, stack=sa)
            yT = cx.sbuf("fyT", 128, 2048, BF16, stack=sa)
            vbb = cx.sbuf("vbb", 128, 520, F32, stack=sa)
            alloc_ln_bufs(cx, sa, "r3", 256)
            for n in range(16):
                pb.op("vector", lambda e, n=n: e.memset(va[n][:, :], 1.0), w=(f"fva{n}",))

            for wh in range(2):
                pb.dma("gpsimd", wfull[:, :],
                       bass.AP(cx.dram["l1f_wk"], 512 * wh,
                               [[1024, 128], [128 * 1024, 8], [1, 512]]),
                       w=("W:wfull",), tag=f"fldk{wh}")
                for oc in range(4):
                    occ = 4 * wh + oc
                    for tb in range(4):
                        ps, pk = psCD[tb % 2]
                        for c in range(NCH):
                            top(cx, lambda e, c=c, oc=oc, tb=tb, ps=ps:
                                e.matmul(ps[:, 0:512], wfull[:, 512 * c + 128 * oc:512 * c + 128 * oc + 128],
                                         cx.sb[f"xTb{c}"][:, 512 * tb:512 * tb + 512],
                                         start=(c == 0), stop=(c == NCH - 1)),
                                r=("W:wfull", f"xTb{c}"), w=(pk,), tag=f"fk{occ}_{tb}")
                        vop(cx, lambda e, occ=occ, tb=tb, ps=ps:
                            e.tensor_scalar(kT[occ][:, 512 * tb:512 * tb + 512], ps[:, 0:512],
                                            pp[:, 8 + occ:9 + occ], None, ALU.add),
                            r=(pk, "pp"), w=(f"fkT{occ}",), wd=512, tag=f"fke{occ}_{tb}")
            for wh in range(2):
                pb.dma("sync", vbr[0:1, :], cx.dram["l1f_vbr"][0:1, 520 * wh:520 * wh + 520],
                       w=("vbr",), tag=f"ldvbr{wh}")
                for seg, w_ in ((0, 512), (512, 8)):
                    top(cx, lambda e, seg=seg, w_=w_:
                        e.matmul(psE[:, 0:w_], cx.sb["ones_f"][0:1, :],
                                 vbr[0:1, seg:seg + w_], start=True, stop=True),
                        r=("vbr", "ones_f"), w=("psE",), tag=f"fvb{wh}_{seg}")
                    vop(cx, lambda e, seg=seg, w_=w_: e.tensor_copy(vbb[:, seg:seg + w_], psE[:, 0:w_]),
                        r=("psE",), w=("vbb",), wd=512, tag=f"fvbc{wh}_{seg}")
                pb.dma("gpsimd", wfull[:, :],
                       bass.AP(cx.dram["l1f_wv"], 512 * wh,
                               [[1024, 128], [128 * 1024, 8], [1, 512]]),
                       w=("W:wfull",), tag=f"fldv{wh}")
                for tc_i in range(16):
                    ps, pk = psCD[tc_i % 2]
                    for hh in range(8):
                        for c in range(NCH):
                            top(cx, lambda e, tc_i=tc_i, hh=hh, c=c, ps=ps:
                                e.matmul(ps[:, 64 * hh:64 * hh + 64],
                                         cx.sb[f"xTb{c}"][:, 128 * tc_i:128 * tc_i + 128],
                                         wfull[:, 512 * c + 64 * hh:512 * c + 64 * hh + 64],
                                         start=(c == 0), stop=(c == NCH - 1)),
                                r=("W:wfull", f"xTb{c}"), w=(pk,), tag=f"fv{wh}_{tc_i}_{hh}")
                    vop(cx, lambda e, tc_i=tc_i, wh=wh, ps=ps: e.tensor_tensor(
                            bass.AP(va[tc_i], 65 * 8 * wh, [[1040, 128], [65, 8], [1, 64]]),
                            bass.AP(ps, 0, [[512, 128], [64, 8], [1, 64]]),
                            bass.AP(vbb, 0, [[520, 128], [65, 8], [1, 64]]), ALU.add),
                        r=(pk, "vbb"), w=(f"fva{tc_i}",), wd=512, tag=f"fve{wh}_{tc_i}")
            for blk in range(8):
                sb0 = 256 * blk
                for wh in range(2):
                    pb.dma("gpsimd", wfull[:, :],
                           bass.AP(cx.dram["l1f_wq"], 512 * wh,
                                   [[1024, 128], [128 * 1024, 8], [1, 512]]),
                           w=("W:wfull",), tag=f"fldq{blk}_{wh}")
                    for oc in range(4):
                        occ = 4 * wh + oc
                        ps, pk = psCD[oc % 2]
                        for c in range(NCH):
                            top(cx, lambda e, c=c, oc=oc, ps=ps, sb0=sb0:
                                e.matmul(ps[:, 0:256], wfull[:, 512 * c + 128 * oc:512 * c + 128 * oc + 128],
                                         cx.sb[f"xTb{c}"][:, sb0:sb0 + 256],
                                         start=(c == 0), stop=(c == NCH - 1)),
                                r=("W:wfull", f"xTb{c}"), w=(pk,), tag=f"fq{blk}_{occ}")
                        vop(cx, lambda e, occ=occ, ps=ps:
                            e.tensor_scalar(qT[:, 256 * occ:256 * occ + 256], ps[:, 0:256],
                                            pp[:, occ:occ + 1], None, ALU.add),
                            r=(pk, "pp"), w=("fqT",), wd=256, tag=f"fqe{blk}_{occ}")
                for hh in range(16):
                    attn_head_block(cx, AB, qT, "fqT",
                                    (lambda n, hh=hh: kT[hh // 2][64 * (hh % 2):64 * (hh % 2) + 64,
                                                                  128 * n:128 * n + 128]),
                                    f"fkT{hh // 2}", va, "fva", list(range(16)), -100,
                                    yT, "fyT", hh, SCL, f"fa{blk}_{hh}", W=256)
                banksA = [(psA0[:, 0:256], "psA0"), (psA0[:, 512:768], "psA0"),
                          (psA1[:, 0:256], "psA1"), (psB[:, 0:256], "psB"),
                          (psB[:, 512:768], "psB"), (psC[:, 0:256], "psC"),
                          (psD[:, 0:256], "psD"), (psE[:, 0:256], "psE")]
                for wh in range(2):
                    pb.dma("gpsimd", wfull[:, :],
                           bass.AP(cx.dram["l1f_wo"], 512 * wh,
                                   [[1024, 128], [128 * 1024, 8], [1, 512]]),
                           w=("W:wfull",), tag=f"fldo{blk}_{wh}")
                    for oc in range(4):
                        occ = 4 * wh + oc
                        pps, ppk = banksA[occ]
                        for c in range(NCH):
                            top(cx, lambda e, c=c, oc=oc, pps=pps:
                                e.matmul(pps, wfull[:, 512 * c + 128 * oc:512 * c + 128 * oc + 128],
                                         yT[:, 256 * c:256 * c + 256], start=(c == 0), stop=(c == NCH - 1)),
                                r=("W:wfull", "fyT"), w=(ppk,), tag=f"fo{blk}_{occ}")
                for occ in range(NCH):
                    pps, ppk = banksA[occ]
                    vop(cx, lambda e, occ=occ, pps=pps, sb0=sb0:
                        e.scalar_tensor_tensor(cx.sb[f"r3{occ}"][:, 0:256], pps,
                                               pp[:, 24 + occ:25 + occ],
                                               cx.sb[f"xT{occ}"][:, sb0:sb0 + 256],
                                               ALU.add, ALU.add),
                        r=(ppk, "pp", f"xT{occ}"), w=(f"r3{occ}",), wd=256, tag=f"fre{blk}_{occ}")
                emit_postln(cx, "r3", sb0, 256, (psC, "psC"), (psD, "psD"),
                            (psA0, "psA0"), (psB, "psB"), tag=f"f1{blk}",
                            gb=(g1, b1_, "pp"),
                            dst=(lambda c2, sb0=sb0: (cx.sb[f"xTb{c2}"][:, sb0:sb0 + 256],
                                                      f"xTb{c2}")))
                for c in range(NCH):
                    top(cx, lambda e, c=c:
                        e.matmul(psE[0:1, 0:256], rwg[:, c:c + 1],
                                 cx.sb[f"r3{c}"][:, 0:256],
                                 start=(c == 0), stop=(c == NCH - 1)),
                        r=("rwg", f"r3{c}"), w=("psE",), tag=f"kA{blk}_{c}")
                mrow = cx.sb["r3_rows"]
                vop(cx, lambda e, krowa=krowa, mrow=mrow: e.tensor_scalar(
                        krowa[0:1, 0:256], mrow[0:1, 0:256], kbc[0:1, 0:1], None, ALU.mult),
                    r=("r3_rows", "kbc"), w=("krow",), tag=f"kk1{blk}")
                vop(cx, lambda e, krowa=krowa: e.tensor_add(krowa[0:1, 0:256], krowa[0:1, 0:256],
                                                            psE[0:1, 0:256]),
                    r=("krow", "psE"), w=("krow",), tag=f"kk2{blk}")
                vop(cx, lambda e, krowa=krowa, mrow=mrow: e.tensor_mul(
                        krowa[0:1, 0:256], krowa[0:1, 0:256], mrow[0:1, 512:768]),
                    r=("krow", "r3_rows"), w=("krow",), tag=f"kk3{blk}")
                vop(cx, lambda e, krowa=krowa: e.tensor_scalar(krowa[0:1, 0:256], krowa[0:1, 0:256],
                                                               kbc[0:1, 1:2], None, ALU.add),
                    r=("krow", "kbc"), w=("krow",), tag=f"kk4{blk}")
                pb.dma("sync", cx.dram["scrk"][0:1, sb0:sb0 + 256], krowa[0:1, 0:256],
                       r=("krow",), w=("scrk",), tag=f"kst{blk}")

        pb.barrier()
        with ExitStack() as sb_:
            merow = cx.sbuf("merow", 1, S, BF16, stack=sb_)
            me4p = cx.sbuf("me4p", 4, S, BF16, stack=sb_)
            merep = cx.sbuf("merep", 128, 512, F32, stack=sb_)
            mrow1 = cx.sbuf("mrow1", 1, 512, F32, stack=sb_)
            with ExitStack() as sk:
                krowb = cx.sbuf("krowB", 1, S, F32, stack=sk)
                mrow2 = cx.sbuf("mrow2", 1, 512, F32, stack=sk)
                mrow3 = cx.sbuf("mrow3", 1, 512, F32, stack=sk)
                mirow = cx.sbuf("mirow", 1, 512, mybir.dt.int32, stack=sk)
                pb.dma("sync", krowb[0:1, :], cx.dram["scrk"][0:1, :], r=("scrk",), w=("krow",), tag="kld")
                for qb in range(4):
                    q0 = 512 * qb
                    vop(cx, lambda e, q0=q0, krowb=krowb: e.tensor_copy(
                            mirow[0:1, :], krowb[0:1, q0:q0 + 512]),
                        r=("krow",), w=("mirow",), tag=f"fl1_{qb}")
                    vop(cx, lambda e: e.tensor_copy(mrow1[0:1, :], mirow[0:1, :]),
                        r=("mirow",), w=("mrow1",), tag=f"fl2_{qb}")
                    vop(cx, lambda e, q0=q0, krowb=krowb: e.tensor_tensor(
                            mrow3[0:1, :], mrow1[0:1, :], krowb[0:1, q0:q0 + 512], ALU.is_gt),
                        r=("mrow1", "krow"), w=("mrow3",), tag=f"fl3_{qb}")
                    vop(cx, lambda e: e.tensor_sub(mrow1[0:1, :], mrow1[0:1, :], mrow3[0:1, :]),
                        r=("mrow1", "mrow3"), w=("mrow1",), tag=f"fl4_{qb}")
                    vop(cx, lambda e: e.tensor_scalar(mrow3[0:1, :], mrow1[0:1, :], 0.25, None, ALU.mult),
                        r=("mrow1",), w=("mrow3",), tag=f"fl5_{qb}")
                    vop(cx, lambda e: e.tensor_copy(mirow[0:1, :], mrow3[0:1, :]),
                        r=("mrow3",), w=("mirow",), tag=f"fl6_{qb}")
                    vop(cx, lambda e: e.tensor_copy(mrow2[0:1, :], mirow[0:1, :]),
                        r=("mirow",), w=("mrow2",), tag=f"fl7_{qb}")
                    vop(cx, lambda e: e.tensor_tensor(mrow3[0:1, :], mrow2[0:1, :], mrow3[0:1, :],
                                                      ALU.is_gt),
                        r=("mrow2", "mrow3"), w=("mrow3",), tag=f"fl8_{qb}")
                    vop(cx, lambda e: e.tensor_sub(mrow2[0:1, :], mrow2[0:1, :], mrow3[0:1, :]),
                        r=("mrow2", "mrow3"), w=("mrow2",), tag=f"fl9_{qb}")
                    vop(cx, lambda e, q0=q0: e.scalar_tensor_tensor(merow[0:1, q0:q0 + 512],
                                                                    mrow2[0:1, :], -4.0, mrow1[0:1, :],
                                                                    ALU.mult, ALU.add),
                        r=("mrow2", "mrow1"), w=("merow",), tag=f"flA_{qb}")
                    for ei in range(4):
                        vop(cx, lambda e, ei=ei, q0=q0: e.tensor_scalar(
                                mrow2[0:1, :], merow[0:1, q0:q0 + 512],
                                float(ei), None, ALU.is_equal),
                            r=("merow",), w=("mrow2",), tag=f"me{ei}_{qb}")
                        pb.dma("gpsimd", me4p[ei:ei + 1, q0:q0 + 512], mrow2[0:1, :],
                               r=("mrow2",), w=("me4p",), tag=f"mep{ei}_{qb}")
            pb.barrier()
            b1c = cx.sbuf("b1c", 128, 4 * 32, F32, stack=sb_)
            b2t = cx.sbuf("b2t", 4, 1024, BF16, stack=sb_)
            w1b = cx.sbuf("w1b", 128, 2 * 1024, BF16, stack=sb_)
            w2b = cx.sbuf("w2b", 128, 4096, BF16, stack=sb_)
            he = cx.sbuf("he", 128, 32 * 512, BF16, stack=sb_)
            hrel = cx.sbuf("hrel", 128, 512, F32, stack=sb_)
            fc = [cx.sbuf(f"fc{c}", 128, 512, F32, stack=sb_) for c in range(NCH)]
            alloc_ln_bufs(cx, sb_, "r4", 256)
            pb.dma("sync", b1c[:, :], cx.dram["l1f_b1c"][:, :], w=("b1c",), tag="ldb1c")
            pb.dma("gpsimd", b2t[:, :], cx.dram["l1f_b2t"][:, :], w=("b2t",), tag="ldb2t")
            for blk in range(4):
                sb0 = 512 * blk
                for ei in range(4):
                    vop(cx, lambda e, ei=ei, sb0=sb0: e.tensor_scalar(
                            mrow1[0:1, 0:512], merow[0:1, sb0:sb0 + 512],
                            float(ei), None, ALU.is_equal),
                        r=("merow",), w=("mrow1",), tag=f"mrr{blk}_{ei}")
                    top(cx, lambda e: e.matmul(psE[:, 0:512], cx.sb["ones_f"][0:1, :],
                                               mrow1[0:1, 0:512], start=True, stop=True),
                        r=("mrow1", "ones_f"), w=("psE",), tag=f"mrm{blk}_{ei}")
                    vop(cx, lambda e: e.tensor_copy(merep[:, :], psE[:, 0:512]),
                        r=("psE",), w=("merep",), wd=512, tag=f"mrc{blk}_{ei}")
                    for hm in range(32):
                        wsl = hm % 2
                        pb.dma("gpsimd", w1b[:, 1024 * wsl:1024 * wsl + 1024],
                               bass.AP(cx.dram["l1f_w1"], ei * 1024 * 4096 + 128 * hm,
                                       [[4096, 128], [128 * 4096, 8], [1, 128]]),
                               w=(f"W:w1b{wsl}",), tag=f"fw1{blk}_{ei}_{hm}")
                        ps, pk = psCD[hm % 2]
                        for c in range(NCH):
                            top(cx, lambda e, c=c, hm=hm, ps=ps, wsl=wsl, sb0=sb0:
                                e.matmul(ps[:, 0:512], w1b[:, 1024 * wsl + 128 * c:1024 * wsl + 128 * c + 128],
                                         cx.sb[f"xTb{c}"][:, sb0:sb0 + 512],
                                         start=(c == 0), stop=(c == NCH - 1)),
                                r=(f"W:w1b{wsl}", f"xTb{c}"), w=(pk,), tag=f"h1{blk}_{ei}_{hm}")
                        sop(cx, lambda e, hm=hm, ei=ei, ps=ps:
                            e.activation(hrel[:, :], ps[:, 0:512], AF.Relu,
                                         bias=b1c[:, 32 * ei + hm:32 * ei + hm + 1]),
                            r=(pk, "b1c"), w=("hrel",), wd=512, tag=f"h1r{blk}_{ei}_{hm}")
                        vop(cx, lambda e, hm=hm: e.tensor_mul(he[:, 512 * hm:512 * hm + 512],
                                                              hrel[:, :], merep[:, :]),
                            r=("hrel", "merep"), w=("he",), wd=512, tag=f"h1m{blk}_{ei}_{hm}")
                    for cc in range(NCH):
                        pb.dma("gpsimd", w2b[:, :],
                               bass.AP(cx.dram["l1f_w2"], ei * 4096 * 1024 + 128 * cc,
                                       [[1024, 128], [128 * 1024, 32], [1, 128]]),
                               w=("W:w2b",), tag=f"fw2{blk}_{ei}_{cc}")
                        pw, pwk = [(psA0[:, 0:512], "psA0"), (psA0[:, 512:1024], "psA0"),
                                   (psB[:, 0:512], "psB"), (psB[:, 512:1024], "psB")][cc % 4]
                        for hm in range(32):
                            top(cx, lambda e, hm=hm, pw=pw, ei=ei:
                                e.matmul(pw, w2b[:, 128 * hm:128 * hm + 128],
                                         he[:, 512 * hm:512 * hm + 512],
                                         start=(hm == 0), stop=(hm == 31 and ei != 3)),
                                r=("W:w2b", "he"), w=(pwk,), tag=f"h2{blk}_{ei}_{cc}_{hm}")
                        if ei == 3:
                            top(cx, lambda e, cc=cc, pw=pw, sb0=sb0:
                                e.matmul(pw, b2t[:, 128 * cc:128 * cc + 128],
                                         me4p[:, sb0:sb0 + 512], start=False, stop=True),
                                r=("b2t", "me4p"), w=(pwk,), tag=f"b2{blk}_{cc}")
                        if ei == 0:
                            vop(cx, lambda e, cc=cc, pw=pw: e.tensor_copy(fc[cc][:, :], pw),
                                r=(pwk,), w=(f"fc{cc}",), wd=512, tag=f"fca{blk}_{ei}_{cc}")
                        else:
                            vop(cx, lambda e, cc=cc, pw=pw: e.tensor_add(fc[cc][:, :], fc[cc][:, :], pw),
                                r=(pwk, f"fc{cc}"), w=(f"fc{cc}",), wd=512, tag=f"fca{blk}_{ei}_{cc}")
                if cx.cfg.get("debug") and blk == 0:
                    cx.dbg("fc0", fc[0][:, :], ("fc0",))
                    cx.dbg("he0", he[:, 0:512], ("he",))
                for sub in range(2):
                    c0s = sb0 + 256 * sub
                    for c in range(NCH):
                        vop(cx, lambda e, c=c, c0s=c0s, sub=sub:
                            e.tensor_add(cx.sb[f"r4{c}"][:, 0:256],
                                         cx.sb[f"xTb{c}"][:, c0s:c0s + 256],
                                         fc[c][:, 256 * sub:256 * sub + 256]),
                            r=(f"xTb{c}", f"fc{c}"), w=(f"r4{c}",), wd=256, tag=f"r4a{blk}_{sub}_{c}")
                    emit_postln(cx, "r4", c0s, 256, (psC, "psC"), (psD, "psD"),
                                (psA0, "psA0"), (psB, "psB"), tag=f"f2{blk}_{sub}",
                                gb=(g2, b2_, "pp"),
                                dst=(lambda c2: (cx.sb[f"r4{c2}"][:, 0:256], f"r4{c2}")))
                    for c in range(NCH):
                        vop(cx, lambda e, c=c, c0s=c0s:
                            e.tensor_add(cx.sb[f"r4{c}"][:, 0:256],
                                         cx.sb[f"xT{c}"][:, c0s:c0s + 256],
                                         cx.sb[f"r4{c}"][:, 0:256]),
                            r=(f"xT{c}", f"r4{c}"), w=(f"r4{c}",), wd=256, tag=f"r5a{blk}_{sub}_{c}")
                    emit_postln(cx, "r4", c0s, 256, (psC, "psC"), (psD, "psD"),
                                (psA0, "psA0"), (psB, "psB"), tag=f"f3{blk}_{sub}")



LAYERS = {"l0a": emit_l0_attn, "l0f": emit_l0_ffn, "l1a": emit_l1_attn, "l1f": emit_l1_ffn,
          "l2a": emit_l2_attn, "l2f": emit_l2_ffn, "l3a": emit_l3_attn, "l3f": emit_l3_ffn}




def host_prep_consts():
    return {"c_ident": np.eye(128, dtype=np.float32)}


def host_prep(inputs, core, consts):
    m = dict(consts)
    x = np.asarray(inputs["L0_attn_x"])
    m["xt_in"] = np.ascontiguousarray(x[core].T.astype(np.float32))
    wq = np.asarray(inputs["L0_attn_Wq"]).T.astype(np.float32)
    wk = np.asarray(inputs["L0_attn_Wk"]).T.astype(np.float32)
    wv = np.asarray(inputs["L0_attn_Wv"]).T.astype(np.float32)
    pk = np.empty((16, 8, 128, 192), np.float32)
    for h in range(16):
        for c in range(8):
            rows = slice(128 * c, 128 * c + 128)
            cols = slice(64 * h, 64 * h + 64)
            pk[h, c, :, 0:64] = wq[rows, cols]
            pk[h, c, :, 64:128] = wk[rows, cols]
            pk[h, c, :, 128:192] = wv[rows, cols]
    m["l0_wqkv"] = np.ascontiguousarray(pk.reshape(16 * 8 * 128, 192))
    m["l0_wp_t"] = np.ascontiguousarray(np.asarray(inputs["L0_attn_Wp"]).T)
    pos = np.asarray(inputs["L0_attn_pos_emb_k"]).astype(np.float32)
    m["l0_posT"] = np.ascontiguousarray(pos.T)
    m["l0_posL"] = np.ascontiguousarray(pos[128][:, None])
    m["l0_posR"] = np.ascontiguousarray(pos[127][:, None])
    m["l0_rw_t"] = np.ascontiguousarray(np.asarray(inputs["L0_ffn_router_w"]).T)
    m["l0_cb"] = np.ascontiguousarray(np.asarray(inputs["L0_ffn_correction_bias"])[None, :])
    m["l0_sgup_t"] = np.ascontiguousarray(np.asarray(inputs["L0_ffn_shared_gate_w"]).T)
    m["l0_su_t"] = np.ascontiguousarray(np.asarray(inputs["L0_ffn_shared_up_w"]).T)
    m["l0_sdw_t"] = np.ascontiguousarray(np.asarray(inputs["L0_ffn_shared_down_w"]).T)
    m["l0_gw"] = np.ascontiguousarray(np.asarray(inputs["L0_ffn_gate_w"]).reshape(16 * 1024, 256))
    m["l0_uw"] = np.ascontiguousarray(np.asarray(inputs["L0_ffn_up_w"]).reshape(16 * 1024, 256))
    m["l0_dw"] = np.ascontiguousarray(np.asarray(inputs["L0_ffn_down_w"]).reshape(16 * 256, 1024))
    oh = np.zeros((16, 16 * 128), np.float32)
    for ei in range(16):
        oh[ei, 128 * ei:128 * ei + 128] = 1.0
    m["c_oh16"] = oh
    m["l1_wlqkv"] = np.ascontiguousarray(np.asarray(inputs["L1_attn_w_lqkv"]))
    m["l1_wgqkv"] = np.ascontiguousarray(np.asarray(inputs["L1_attn_w_gqkv"]))
    m["l1_wlp"] = np.ascontiguousarray(np.asarray(inputs["L1_attn_w_lproj"]))
    m["l1_wgp"] = np.ascontiguousarray(np.asarray(inputs["L1_attn_w_gproj"]))
    m["l1_wgate"] = np.ascontiguousarray(np.asarray(inputs["L1_attn_w_gate"]))
    m["l1_wcomp"] = np.ascontiguousarray(np.asarray(inputs["L1_attn_w_comp"]))
    blq = np.asarray(inputs["L1_attn_b_lqkv"]).astype(np.float32)
    bgq = np.asarray(inputs["L1_attn_b_gqkv"]).astype(np.float32)
    bco = np.asarray(inputs["L1_attn_b_comp"]).astype(np.float32)
    blp = np.asarray(inputs["L1_attn_b_lproj"]).astype(np.float32)
    bgp = np.asarray(inputs["L1_attn_b_gproj"]).astype(np.float32)
    bb = np.zeros((128, 40), np.float32)
    for j in range(12):
        bb[:, j] = blq[128 * j:128 * j + 128]
        bb[:, 12 + j] = bgq[128 * j:128 * j + 128]
    for j in range(8):
        bb[:, 24 + j] = bco[128 * j:128 * j + 128]
    for j in range(4):
        bb[:, 32 + j] = blp[128 * j:128 * j + 128]
        bb[:, 36 + j] = bgp[128 * j:128 * j + 128]
    m["l1_biases"] = bb
    vb = np.zeros((1, 1040), np.float32)
    for hh8 in range(8):
        vb[0, 65 * hh8:65 * hh8 + 64] = blq[1024 + 64 * hh8:1024 + 64 * hh8 + 64]
        vb[0, 520 + 65 * hh8:520 + 65 * hh8 + 64] = bgq[1024 + 64 * hh8:1024 + 64 * hh8 + 64]
    m["l1_vbias"] = vb
    m["l1_bgate"] = np.asarray(inputs["L1_attn_b_gate"]).astype(np.float32)[None, :]
    tri = np.where(np.arange(128)[:, None] <= np.arange(128)[None, :], 0.0, -1e30).astype(np.float32)
    m["c_tri"] = tri
    m["c_rep4"] = (np.arange(512)[None, :] // 4 == np.arange(128)[:, None]).astype(np.float32)
    ppf = np.zeros((128, 66), np.float32)
    for j, nmk in enumerate(["bq", "bk", "bv", "bo"]):
        bvv = np.asarray(inputs[f"L1_ffn_{nmk}"]).astype(np.float32)
        for cchunk in range(8):
            ppf[:, 8 * j + cchunk] = bvv[128 * cchunk:128 * cchunk + 128]
    for j, nmk in enumerate(["ln1_g", "ln1_b", "ln2_g", "ln2_b"]):
        bvv = np.asarray(inputs[f"L1_ffn_{nmk}"]).astype(np.float32)
        for cchunk in range(8):
            ppf[:, 32 + 8 * j + cchunk] = bvv[128 * cchunk:128 * cchunk + 128]
    ppf[:, 64] = float(np.asarray(inputs["L1_ffn_router_b"])[0])
    m["l1f_pp"] = ppf
    bvq = np.asarray(inputs["L1_ffn_bv"]).astype(np.float32)
    vbrow = np.zeros((1, 1040), np.float32)
    for h16 in range(16):
        vbrow[0, 65 * h16:65 * h16 + 64] = bvq[64 * h16:64 * h16 + 64]
    m["l1f_vbr"] = vbrow
    m["l1f_wq"] = np.ascontiguousarray(np.asarray(inputs["L1_ffn_Wq"]))
    m["l1f_wk"] = np.ascontiguousarray(np.asarray(inputs["L1_ffn_Wk"]))
    m["l1f_wv"] = np.ascontiguousarray(np.asarray(inputs["L1_ffn_Wv"]))
    m["l1f_wo"] = np.ascontiguousarray(np.asarray(inputs["L1_ffn_Wo"]))
    m["l1f_rwc"] = np.ascontiguousarray(np.asarray(inputs["L1_ffn_router_w"]).astype(np.float32).reshape(8, 128).T)
    b1a = np.asarray(inputs["L1_ffn_b1"]).astype(np.float32)
    b1c = np.zeros((128, 128), np.float32)
    for e4 in range(4):
        for hm in range(32):
            b1c[:, 32 * e4 + hm] = b1a[e4, 128 * hm:128 * hm + 128]
    m["l1f_b1c"] = b1c
    m["l1f_b2t"] = np.ascontiguousarray(np.asarray(inputs["L1_ffn_b2"]).astype(np.float32))
    m["l1f_w1"] = np.ascontiguousarray(np.asarray(inputs["L1_ffn_W1"]).reshape(4 * 1024, 4096))
    m["l1f_w2"] = np.ascontiguousarray(np.asarray(inputs["L1_ffn_W2"]).reshape(4 * 4096, 1024))
    for wn, dn in (("l2_wq", "L2_attn_Wq"), ("l2_wk", "L2_attn_Wk"), ("l2_wv", "L2_attn_Wv"),
                   ("l2_wpost", "L2_attn_Wpost"), ("l2_wrel", "L2_attn_Wrel")):
        m[wn] = np.ascontiguousarray(np.asarray(inputs[dn]).astype(np.float32))
    m["l2_pds"] = np.ascontiguousarray(np.asarray(inputs["L2_attn_per_dim_scale"])
                                       .astype(np.float32).reshape(128, 1))
    m["l2_posT"] = np.ascontiguousarray(np.asarray(inputs["L2_attn_pos_emb"]).astype(np.float32).T)
    WT2 = 160
    msk = np.zeros((128, 16 * WT2), np.float32)
    for mtile in range(16):
        t0p = 12 * ((128 * mtile) // 12)
        for sl_ in range(128):
            cs = 12 * ((128 * mtile + sl_) // 12)
            lo = cs - t0p
            msk[sl_, WT2 * mtile + lo:WT2 * mtile + lo + 24] = 1.0
    m["l2_mask"] = msk
    swg = np.asarray(inputs["L2_ffn_sw_gate"]).astype(np.float32)
    swu = np.asarray(inputs["L2_ffn_sw_up"]).astype(np.float32)
    swd = np.asarray(inputs["L2_ffn_sw_down"]).astype(np.float32)
    wg8 = np.asarray(inputs["L2_ffn_w_gate"]).astype(np.float32)
    wu8 = np.asarray(inputs["L2_ffn_w_up"]).astype(np.float32)
    wd8 = np.asarray(inputs["L2_ffn_w_down"]).astype(np.float32)
    m["l2f_wg"] = np.ascontiguousarray(np.concatenate([swg[None], wg8], 0).reshape(9 * 1024, 1408))
    m["l2f_wu"] = np.ascontiguousarray(np.concatenate([swu[None], wu8], 0).reshape(9 * 1024, 1408))
    m["l2f_wd"] = np.ascontiguousarray(np.concatenate([swd[None], wd8], 0).reshape(9 * 1408, 1024))
    m["l2f_gt"] = np.ascontiguousarray(np.asarray(inputs["L2_ffn_gate_w"]).astype(np.float32).T)
    m["l2f_sgw"] = np.ascontiguousarray(np.asarray(inputs["L2_ffn_shared_gate_w"]).astype(np.float32).T)
    for wn, dn in (("l3_wq", "L3_attn_Wq"), ("l3_wk", "L3_attn_Wk"), ("l3_wv", "L3_attn_Wv"),
                   ("l3_wo", "L3_attn_Wo")):
        m[wn] = np.ascontiguousarray(np.asarray(inputs[dn]).astype(np.float32))
    for bn, dn in (("l3_bq", "L3_attn_bq"), ("l3_bk", "L3_attn_bk"), ("l3_bv", "L3_attn_bv"),
                   ("l3_bo", "L3_attn_bo")):
        m[bn] = np.ascontiguousarray(np.asarray(inputs[dn]).astype(np.float32)[None, :])
    fidx = np.arange(1152)
    sidx = np.arange(2048)
    ang = -2.0 * np.pi * np.outer(sidx, fidx) / 2048.0
    dftc = np.zeros((2048, 2304), np.float32)
    dftc[:, 0:1152] = np.cos(ang)
    dftc[:, 1152:2304] = np.sin(ang)
    dftc[:, 1025:1152] = 0.0
    dftc[:, 1152 + 1025:2304] = 0.0
    m["l3_dftc"] = np.ascontiguousarray(dftc)
    wgt = np.full(1152, 2.0, np.float32)
    wgt[0] = 1.0
    wgt[1024] = 1.0
    wgt[1025:] = 0.0
    ang2 = 2.0 * np.pi * np.outer(fidx, sidx) / 2048.0
    icis = np.zeros((2304, 2048), np.float32)
    icis[0:1152] = np.cos(ang2) * wgt[:, None] / 2048.0
    icis[1152:2304] = -np.sin(ang2) * wgt[:, None] / 2048.0
    icis[1025:1152] = 0.0
    icis[1152 + 1025:2304] = 0.0
    m["l3_icis"] = np.ascontiguousarray(icis)
    m["l3f_w1t"] = np.ascontiguousarray(np.asarray(inputs["L3_ffn_w1_w"]).astype(np.float32).T)
    m["l3f_w2t"] = np.ascontiguousarray(np.asarray(inputs["L3_ffn_w2_w"]).astype(np.float32).T)
    m["l3f_h1t"] = np.ascontiguousarray(np.asarray(inputs["L3_ffn_h1_w"]).astype(np.float32).T)
    m["l3f_h2t"] = np.ascontiguousarray(np.asarray(inputs["L3_ffn_h2_w"]).astype(np.float32).T)
    m["l3f_w3t"] = np.ascontiguousarray(np.asarray(inputs["L3_ffn_w3_w"]).astype(np.float32).T)
    m["l3f_s12"] = np.ascontiguousarray(np.concatenate(
        [np.asarray(inputs["L3_ffn_w1_s"]).astype(np.float32).reshape(32, 128).T,
         np.asarray(inputs["L3_ffn_w2_s"]).astype(np.float32).reshape(32, 128).T], 1))
    m["l3f_sh"] = np.ascontiguousarray(np.concatenate(
        [np.asarray(inputs["L3_ffn_h1_s"]).astype(np.float32).reshape(32, 128).T,
         np.asarray(inputs["L3_ffn_h2_s"]).astype(np.float32).reshape(32, 128).T], 1))
    m["l3f_s3"] = np.ascontiguousarray(np.asarray(inputs["L3_ffn_w3_s"]).astype(np.float32).reshape(8, 128).T)
    return m


def kernel_partial(inputs, layers, cores, debug=False):
    consts = host_prep_consts()
    in_maps = [host_prep(inputs, c, consts) for c in cores]
    spec = {k: (tuple(v.shape), str(v.dtype)) for k, v in in_maps[0].items()}
    cfg = {"inputs_spec": spec, "layers": layers, "debug": debug}
    nc = build_program(cfg)
    res = run_bass_kernel_spmd(nc, in_maps, core_ids=list(range(len(cores))))
    outs = [np.asarray(r["out"]).T for r in res.results]
    if cfg.get("debug"):
        return outs, res.results
    return outs


ALL_LAYERS = ["l0a", "l0f", "l1a", "l1f", "l2a", "l2f", "l3a", "l3f"]


def kernel(**inputs):
    outs = kernel_partial(inputs, ALL_LAYERS, list(range(8)))
    return np.stack(outs).astype(np.float32)
